# version 03 (final)
# baseline (speedup 1.0000x reference)
import numpy as np
import ml_dtypes
from contextlib import ExitStack

import concourse.bass as bass
import concourse.mybir as mybir
from concourse.bass_utils import run_bass_kernel_spmd

F32 = mybir.dt.float32
F16 = mybir.dt.float16
I32 = mybir.dt.int32
AF = mybir.ActivationFunctionType
ALU = mybir.AluOpType
AX = mybir.AxisListType

D = 2048
NCH = 16
TOK = 2048
NTC = 16
S = 4096
B = 4
NCORES = 8
H = 16
HD = 128
ND = 64
RD = 64
R_KV = 512
H_KV = 4
THETA = 10000.0
SCALE = 1.0 / float(np.sqrt(128.0))
EPS_LN = 1e-5
NE = 32
TOPK = 8
G = 8
I_FF = 768
SPIN_CAP = 200_000_000
L0_STAGE = 99
NAME_SALT = ""
L0_SUB = 9
EXP_SHIFT = 10.0

ENGINES = ("sync", "scalar", "vector", "tensor", "gpsimd")

_SRC_HASH = None


def mk_nc(tag):
    global _SRC_HASH
    if _SRC_HASH is None:
        import hashlib
        try:
            with open(__file__, "rb") as f:
                _SRC_HASH = hashlib.sha256(f.read()).hexdigest()[:10]
        except OSError:
            import time
            _SRC_HASH = str(int(time.time() * 1000) % 10**9)
    return bass.Bass("TRN2", target_bir_lowering=False, debug=False,
                     name=f"{tag}_{NAME_SALT}{_SRC_HASH}")



def own_globals(parity):
    return [2 * j + parity for j in range(NTC)]


def trips_for_super(s):
    return 8 * s + 8


def jmin_for(s, kc):
    if kc <= 8 * s + 1:
        return 0
    return min(3, (kc - 8 * s - 1 + 1) // 2)


def host_masks(parity):
    masks = np.ones((4, 8, 128, 128), dtype=np.float16)
    for s in range(4):
        for e in range(8):
            kc = 8 * s + e
            j = jmin_for(s, kc)
            g = 8 * s + 2 * j + parity
            if kc < g:
                continue
            elif kc == g:
                k_idx = np.arange(128)[:, None]
                q_idx = np.arange(128)[None, :]
                masks[s, e] = (k_idx <= q_idx).astype(np.float16)
            else:
                masks[s, e] = 0.0
    return masks


def host_cos_sin(parity):
    gl = own_globals(parity)
    pos = np.concatenate([np.arange(g * 128, (g + 1) * 128) for g in gl])
    freqs = 1.0 / THETA ** (np.arange(0, RD, 2, dtype=np.float64) / RD)
    ang = pos[None, :].astype(np.float64) * freqs[:, None]
    cos = np.repeat(np.cos(ang), 2, axis=0)
    sin = np.repeat(np.sin(ang), 2, axis=0)
    return cos.astype(np.float16), sin.astype(np.float16)


def host_perm_T():
    P = np.zeros((RD, RD), dtype=np.float16)
    for i in range(RD // 2):
        P[2 * i, 2 * i + 1] = -1.0
        P[2 * i + 1, 2 * i] = 1.0
    return P.T.copy()


def shard_rows(x_b_s_d, parity_pairs=True):
    shards = []
    for c in range(NCORES):
        b, h = c // 2, c % 2
        gl = own_globals(h)
        rows = np.concatenate([x_b_s_d[b, g * 128:(g + 1) * 128, :] for g in gl], axis=0)
        shards.append(np.ascontiguousarray(rows))
    return shards


def unshard_rows(shards, dtype=np.float32):
    out = np.zeros((B, S, D), dtype=dtype)
    for c in range(NCORES):
        b, h = c // 2, c % 2
        gl = own_globals(h)
        for j, g in enumerate(gl):
            out[b, g * 128:(g + 1) * 128, :] = shards[c][j * 128:(j + 1) * 128, :]
    return out



class Plan:
    def __init__(self):
        self.ops = {e: [] for e in ENGINES}
        self.cnt = {}

    def inc(self, sem, n):
        self.cnt[sem] = self.cnt.get(sem, 0) + n
        return self.cnt[sem]

    def val(self, sem):
        return self.cnt.get(sem, 0)

    def add(self, engine, fn):
        self.ops[engine].append(fn)

    def emit(self, block, sems):
        def run(eng, lst):
            for f in lst:
                f(eng, sems)
        if self.ops["sync"]:
            block.sync(lambda eng, lst=self.ops["sync"]: run(eng, lst))
        if self.ops["scalar"]:
            block.scalar(lambda eng, lst=self.ops["scalar"]: run(eng, lst))
        if self.ops["vector"]:
            block.vector(lambda eng, lst=self.ops["vector"]: run(eng, lst))
        if self.ops["tensor"]:
            block.tensor(lambda eng, lst=self.ops["tensor"]: run(eng, lst))
        if self.ops["gpsimd"]:
            block.gpsimd(lambda eng, lst=self.ops["gpsimd"]: run(eng, lst))


def _waits(eng, sems, waits):
    for (s, v) in waits:
        eng.wait_ge(sems[s], v)


def p_dma(P, queue, sem, out, in_, waits=(), **kw):
    tgt = P.inc(sem, 16)
    waits = tuple(waits)

    def f(eng, sems, out=out, in_=in_, waits=waits, kw=dict(kw)):
        _waits(eng, sems, waits)
        eng.dma_start(out=out, in_=in_, **kw).then_inc(sems[sem], 16)
    P.add(queue, f)
    return tgt


def p_dma_tr(P, queue, sem, out, in_, waits=()):
    tgt = P.inc(sem, 16)
    waits = tuple(waits)

    def f(eng, sems, out=out, in_=in_, waits=waits):
        _waits(eng, sems, waits)
        eng.dma_start_transpose(out, in_).then_inc(sems[sem], 16)
    P.add(queue, f)
    return tgt


def p_op(P, engine, sem, build, waits=(), n=1):
    tgt = P.inc(sem, n) if sem is not None else None
    waits = tuple(waits)

    def f(eng, sems, build=build, waits=waits):
        _waits(eng, sems, waits)
        ins = build(eng)
        if sem is not None:
            ins.then_inc(sems[sem], n)
    P.add(engine, f)
    return tgt



def emit_const_setup(P, T):
    def vsetup(eng):
        eng.memset(T["ones128"][:], 1.0)
        eng.memset(T["ones1"][:], 1.0)
        eng.memset(T["ones1f32"][:], 1.0)
        eng.memset(T["eps"][:], EPS_LN)
        eng.memset(T["scale_qk"][:], SCALE)
        eng.memset(T["one_f32"][:], 1.0)
        return eng.memset(T["zero_f32"][:], 0.0)
    p_op(P, "vector", "ve", vsetup)

    def gsetup(eng):
        eng.memset(T["ident"][:], 1.0)
        return eng.affine_select(T["ident"][:], T["ident"][:], pattern=[[1, 128]],
                                 compare_op=ALU.is_equal, fill=0.0, base=0,
                                 channel_multiplier=-1)
    p_op(P, "gpsimd", "gpc", gsetup)
    return [("ve", P.val("ve")), ("gpc", P.val("gpc"))]


def p_bounce(P, T, src_ap, dst_ap, waits=()):
    v = p_dma(P, "sync", "sy", dst_ap, src_ap, waits=waits)
    return [("sy", v)]


def emit_ln_stats(P, T, const_ok, src_stream, srcs=None):
    guard = list(const_ok) + [("ve", P.val("ve")), ("ac", P.val("ac"))]
    for tc in range(NTC):
        src = srcs[tc] if srcs is not None else src_stream[:, tc, :]
        p_op(P, "vector", "ve", lambda eng, a=src, o=T["lnsum"][:, tc:tc + 1]:
             eng.tensor_reduce(o, a, AX.X, ALU.add),
             waits=guard if tc == 0 else ())
        p_op(P, "scalar", "ac",
             lambda eng, a=src, o=T["sq_scratch"][:], s=T["lnsumsq"][:, tc:tc + 1]:
             eng.activation(o, a, AF.Square, accum_out=s),
             waits=guard if tc == 0 else ())
    b1 = p_bounce(P, T, T["lnsum"][:], T["lnsum_b"][:],
                  waits=[("ve", P.val("ve"))])
    b2 = p_bounce(P, T, T["lnsumsq"][:], T["lnsumsq_b"][:],
                  waits=[("ac", P.val("ac"))])
    p_op(P, "vector", "ve", lambda eng, T=T:
         eng.tensor_scalar(T["ln_a"][:], T["lnsumsq_b"][:], 1.0 / D, EPS_LN,
                           ALU.mult, ALU.add), waits=b1 + b2)
    p_op(P, "vector", "ve", lambda eng, T=T:
         eng.scalar_tensor_tensor(T["ln_b2"][:], T["lnsum_b"][:], 1.0 / (D * D),
                                  T["lnsum_b"][:], ALU.mult, ALU.mult))
    b3 = p_bounce(P, T, T["ln_a"][:], T["ln_a_b"][:], waits=[("ve", P.val("ve"))])
    b4 = p_bounce(P, T, T["ln_b2"][:], T["ln_b2_b"][:])
    p_op(P, "vector", "ve", lambda eng, T=T:
         eng.tensor_tensor(T["ln_var"][:], T["ln_a_b"][:], T["ln_b2_b"][:],
                           ALU.subtract), waits=b3 + b4)
    b5 = p_bounce(P, T, T["ln_var"][:], T["ln_var_b"][:], waits=[("ve", P.val("ve"))])
    p_op(P, "scalar", "ac", lambda eng, T=T:
         eng.activation(T["ln_std"][:], T["ln_var_b"][:], AF.Sqrt), waits=b5)
    b6 = p_bounce(P, T, T["ln_std"][:], T["ln_std_b"][:], waits=[("ac", P.val("ac"))])
    p_op(P, "vector", "ve", lambda eng, T=T:
         eng.reciprocal(T["rstd_all"][:], T["ln_std_b"][:]), waits=b6)
    b7 = p_bounce(P, T, T["rstd_all"][:], T["rstd_all_b"][:],
                  waits=[("ve", P.val("ve"))])
    p_op(P, "vector", "ve", lambda eng, T=T:
         eng.scalar_tensor_tensor(T["nmr_all"][:], T["lnsum_b"][:], -1.0 / D,
                                  T["rstd_all_b"][:], ALU.mult, ALU.mult), waits=b7)
    b8 = p_bounce(P, T, T["nmr_all"][:], T["nmr_all_b"][:],
                  waits=[("ve", P.val("ve"))])
    return b7 + b8


def emit_ln(P, T, stats_ok, tc, src_ap, dst_tok_ap):
    p_op(P, "scalar", "ac", lambda eng, a=src_ap, o=dst_tok_ap, T=T, tc=tc:
         eng.activation(o, a, AF.Identity, bias=T["nmr_all_b"][:, tc:tc + 1],
                        scale=T["rstd_all_b"][:, tc:tc + 1]),
         waits=list(stats_ok) + [("pe", P.val("pe"))])
    return [("ac", P.val("ac"))]


def emit_tokmajor_to_featT(P, T, src_tok_f16, dst_featT, tc, ready, pt_name="pt_a"):
    outs = []
    for half in range(2):
        pt = T[pt_name]
        def trs(eng, src=src_tok_f16, pt=pt, half=half, T=T):
            last = None
            for i in range(8):
                dch = half * 8 + i
                last = eng.transpose(pt[:, i * 128:(i + 1) * 128],
                                     src[:, dch * 128:(dch + 1) * 128], T["ident"][:])
            return last
        w = (list(ready) if half == 0 else []) + [("ve", P.val("ve"))]
        p_op(P, "tensor", "pe", trs, waits=w)
        pe_v = P.val("pe")
        def ev(eng, pt=pt, dst=dst_featT, half=half, tc=tc):
            return eng.tensor_copy(dst[:, half * 8:(half + 1) * 8, tc * 128:(tc + 1) * 128],
                                   pt[:, 0:1024].rearrange("p (a c) -> p a c", a=8))
        p_op(P, "vector", "ve", ev, waits=[("pe", pe_v)])
        outs.append(("ve", P.val("ve")))
    return [outs[-1]]


def emit_ln_all(P, T, const_ok, src_stream, dst_featT, extra_tok_dst=None):
    stats_ok = emit_ln_stats(P, T, const_ok, src_stream)
    done = None
    for tc in range(NTC):
        r = emit_ln(P, T, stats_ok, tc, src_stream[:, tc, :], T["ln_tok"][:])
        if extra_tok_dst is not None:
            p_op(P, "vector", "ve", lambda eng, s=T["ln_tok"][:],
                 d=extra_tok_dst[:, tc, :]: eng.tensor_copy(d, s), waits=r)
            r = r + [("ve", P.val("ve"))]
        done = emit_tokmajor_to_featT(P, T, T["ln_tok"][:], dst_featT, tc, r)
    return done


def emit_weight_prep_f32(P, T, w_ap, rows, cols, wf16_dram, wT_dram, tag):
    nrs = (rows + 127) // 128
    for rs in range(nrs):
        r0 = rs * 128
        r1 = min(rows, r0 + 128)
        g_v = p_dma(P, "gpsimd", "gp", T["wstage"][0:r1 - r0, 0:cols],
                    w_ap[r0:r1, 0:cols],
                    waits=[("sy", P.val("sy"))])
        p_dma(P, "sync", "sy", wf16_dram[r0:r1, 0:cols],
              T["wstage"][0:r1 - r0, 0:cols], waits=[("gp", g_v)])
    sy_rows = P.val("sy")
    ncs = cols // 128
    for cs in range(ncs):
        t_v = p_dma_tr(P, "sync", "sy", T["wtstage"][:, 0:rows],
                       wf16_dram[0:rows, cs * 128:(cs + 1) * 128],
                       waits=[("sy", sy_rows), ("gp", P.val("gp"))])
        p_dma(P, "gpsimd", "gp", wT_dram[cs * 128:(cs + 1) * 128, 0:rows],
              T["wtstage"][:, 0:rows], waits=[("sy", t_v)])
    return [("gp", P.val("gp"))]


def emit_weight_prep_f16(P, T, w_ap, rows, cols, wT_dram, tag):
    ncs = cols // 128
    for cs in range(ncs):
        t_v = p_dma_tr(P, "sync", "sy", T["wtstage"][:, 0:rows],
                       w_ap[0:rows, cs * 128:(cs + 1) * 128],
                       waits=[("gp", P.val("gp"))])
        p_dma(P, "gpsimd", "gp", wT_dram[cs * 128:(cs + 1) * 128, 0:rows],
              T["wtstage"][:, 0:rows], waits=[("sy", t_v)])
    return [("gp", P.val("gp"))]



def emit_flag_and_spin(P, T, ex_idx, data_done_waits):
    shflag = T["shflag"]

    def f(eng, sems, waits=tuple(data_done_waits), ex=ex_idx):
        _waits(eng, sems, waits)
        with eng.register(f"sl{ex}") as sl:
            eng.reg_load(sl, T["parsb"][0:1, 0:1])
            with eng.If_eq(sl, 0):
                eng.dma_start(out=shflag.ap()[2 * ex + 0][None, 0:1],
                              in_=T["noncesb"][:]).then_inc(sems["gx"], 16)
            with eng.Else():
                eng.dma_start(out=shflag.ap()[2 * ex + 1][None, 0:1],
                              in_=T["noncesb"][:]).then_inc(sems["gx"], 16)
            with eng.register(f"fv{ex}") as fv, eng.register(f"nv{ex}") as nv, \
                 eng.register(f"it{ex}") as it, eng.register(f"go{ex}") as go:
                eng.reg_load(nv, T["noncesb"][0:1, 0:1])
                eng.reg_mov(go, 1)
                eng.reg_mov(it, SPIN_CAP)
                with eng.While(go):
                    with eng.If_eq(sl, 0):
                        eng.load(fv, shflag.ap()[2 * ex + 1][None, 0:1])
                    with eng.Else():
                        eng.load(fv, shflag.ap()[2 * ex + 0][None, 0:1])
                    eng.reg_add(it, it, -1)
                    eng.reg_mov(go, fv)
                    eng.reg_sub(go, go, nv)
                    with eng.If_eq(it, 0):
                        eng.reg_mov(go, 0)
        eng.nop().then_inc(sems["gpc"], 1)
    P.inc("gx", 16)
    P.add("gpsimd", f)
    P.inc("gpc", 1)
    return [("gpc", P.val("gpc"))]


_REG_CTR = [0]


def p_shared_write(P, T, dst_even_ap, dst_odd_ap, src_ap, waits=()):
    tgt = P.inc("gx", 16)
    waits = tuple(waits)
    _REG_CTR[0] += 1
    rn = f"slw{_REG_CTR[0]}"

    def f(eng, sems, de=dst_even_ap, do=dst_odd_ap, s=src_ap, waits=waits, rn=rn):
        _waits(eng, sems, waits)
        with eng.register(rn) as sl:
            eng.reg_load(sl, T["parsb"][0:1, 0:1])
            with eng.If_eq(sl, 0):
                eng.dma_start(out=de, in_=s).then_inc(sems["gx"], 16)
            with eng.Else():
                eng.dma_start(out=do, in_=s).then_inc(sems["gx"], 16)
    P.add("gpsimd", f)
    return tgt



def emit_proj_to_stream(P, T, wT_dram, w_ready, bias_col=None, extra_epilogue=None,
                        n_och=NCH, out_stream=None, kin=NCH, src_featT=None):
    src = src_featT if src_featT is not None else T["xlnT"]
    stream = out_stream if out_stream is not None else T["stream"]
    for och in range(n_och):
        lw = wT_dram.rearrange("(kc p) o -> p kc o", p=128)[:, 0:kin,
                                                           och * 128:(och + 1) * 128]
        buf = T["lhs_slice"]
        ld_v = p_dma(P, "sync", "sy", buf[:, 0:kin, :], lw,
                     waits=w_ready + [("pe", P.val("pe"))])
        for qs in range(4):
            def mm(eng, buf=buf, src=src, qs=qs, kin=kin, T=T):
                last = None
                for k in range(kin):
                    last = eng.matmul(T["ps_a"][:], buf[:, k, :],
                                      src[:, k, qs * 512:(qs + 1) * 512],
                                      start=(k == 0), stop=(k == kin - 1))
                return last
            p_op(P, "tensor", "pe", mm,
                 waits=[("sy", ld_v), ("ac", P.val("ac"))])
            pe_mm = P.val("pe")
            if bias_col is not None:
                def ev(eng, T=T, b=bias_col, och=och):
                    return eng.activation(T["evac16"][:], T["ps_a"][:], AF.Identity,
                                          bias=b[:, och:och + 1])
            else:
                def ev(eng, T=T):
                    return eng.activation(T["evac16"][:], T["ps_a"][:], AF.Identity,
                                          scale=T["one_f32"][:])
            p_op(P, "scalar", "ac", ev, waits=[("pe", pe_mm)])
            ac_v = P.val("ac")
            def trs(eng, T=T):
                last = None
                for i in range(4):
                    last = eng.transpose(T["pt_b"][:, i * 128:(i + 1) * 128],
                                         T["evac16"][:, i * 128:(i + 1) * 128],
                                         T["ident"][:])
                return last
            p_op(P, "tensor", "pe", trs, waits=[("ac", ac_v), ("ve", P.val("ve"))])
            pe_tr = P.val("pe")
            def add(eng, T=T, qs=qs, och=och, stream=stream):
                last = None
                for i in range(4):
                    tcc = qs * 4 + i
                    last = eng.tensor_tensor(
                        stream[:, tcc, och * 128:(och + 1) * 128],
                        T["pt_b"][:, i * 128:(i + 1) * 128],
                        stream[:, tcc, och * 128:(och + 1) * 128], ALU.add)
                return last
            p_op(P, "vector", "ve", add, waits=[("pe", pe_tr)])
    return [("ve", P.val("ve"))]



def alloc_common(nc, ctx, T):
    t = lambda name, shape, dt: ctx.enter_context(nc.sbuf_tensor(name, shape, dt))
    T["stream"] = t("stream", [128, NTC, D], F16)
    T["xlnT"] = t("xlnT", [128, NCH, TOK], F16)
    T["ln_tok"] = t("ln_tok", [128, D], F16)
    T["sq_scratch"] = T["ln_tok"]
    for n in ["st_sum", "st_sumsq", "st_m", "st_msq", "st_m2", "st_var",
              "st_std", "st_rstd", "st_nmr", "eps", "scale_qk", "one_f32",
              "zero_f32", "negM"]:
        T[n] = t(n, [128, 1], F32)
    for n in ["lnsum", "lnsum_b", "lnsumsq", "lnsumsq_b", "ln_a", "ln_a_b",
              "ln_b2", "ln_b2_b", "ln_var", "ln_var_b", "ln_std", "ln_std_b",
              "rstd_all", "rstd_all_b", "nmr_all", "nmr_all_b",
              "thr_all", "thr_all_b"]:
        T[n] = t(n, [128, NTC], F32)
    T["ones128"] = t("ones128", [128, 1], F16)
    T["ones1"] = t("ones1", [1, 128], F16)
    T["ones1f32"] = t("ones1f32", [1, 128], F32)
    T["ident"] = t("ident", [128, 128], F16)
    T["wstage"] = t("wstage", [128, D], F16)
    T["wtstage"] = t("wtstage", [128, D], F16)
    T["lhs_slice"] = t("lhs_slice", [128, NCH, 128], F16)
    T["lhs_slice2"] = t("lhs_slice2", [128, NCH, 128], F16)
    T["evac16"] = t("evac16", [128, 512], F16)
    T["parsb"] = t("parsb", [1, 1], I32)
    T["noncesb"] = t("noncesb", [1, 1], I32)
    p = lambda name, shape, dt: ctx.enter_context(nc.psum_tensor(name, shape, dt))
    T["ps_a"] = p("ps_a", [128, 512], F32)
    T["ps_b"] = p("ps_b", [128, 512], F32)
    T["ps_c"] = p("ps_c", [128, 512], F32)
    T["ps_d"] = p("ps_d", [128, 512], F32)
    T["ps_e"] = p("ps_e", [128, 512], F32)
    T["ps_f"] = p("ps_f", [128, 512], F32)
    T["pt_a"] = p("pt_a", [128, 1024], F16)
    T["pt_b"] = p("pt_b", [128, 1024], F16)


def alloc_sems(nc, ctx, names):
    sems = {}
    for n in names:
        sems[n] = ctx.enter_context(nc.semaphore(n))
    return sems


SEM_NAMES = ["sy", "gp", "gx", "pe", "ve", "ac", "gpc", "sy2", "gp2", "pe2",
             "ve2", "ac2", "st"]


def emit_stream_init(P, T, x_ap):
    v = p_dma(P, "gpsimd", "gp", T["stream"][:],
              x_ap.rearrange("(tc p) d -> p tc d", p=128))
    return [("gp", v)]


def emit_par_nonce_load(P, T, par_ap, nonce_ap):
    a = p_dma(P, "gpsimd", "gp", T["parsb"][:], par_ap)
    b = p_dma(P, "gpsimd", "gp", T["noncesb"][:], nonce_ap)
    return [("gp", b)]


def emit_stream_out(P, T, out_ap, waits):
    v = p_dma(P, "gpsimd", "st", out_ap.rearrange("(tc p) d -> p tc d", p=128),
              T["stream"][:], waits=waits)

    def f(eng, sems, fin=v):
        eng.wait_ge(sems["st"], fin)
    P.add("gpsimd", f)


def alloc_out_stage(nc, ctx, T):
    pass



def emit_l0ffn_ternary_prep(P, T, w_ap, qf16_dram, qT_dram):
    for rs in range(NCH):
        g_v = p_dma(P, "gpsimd", "gp", T["tern_w"][:],
                    w_ap[rs * 128:(rs + 1) * 128, :],
                    waits=[("ve", P.val("ve"))])
        p_op(P, "vector", "ve", lambda eng, T=T, rs=rs:
             eng.tensor_reduce(T["thr_all"][:, rs:rs + 1], T["tern_w"][:], AX.X,
                               ALU.add, apply_absolute_value=True),
             waits=[("gp", g_v)])
    b0 = p_bounce(P, T, T["thr_all"][:], T["thr_all_b"][:],
                  waits=[("ve", P.val("ve"))])
    p_op(P, "vector", "ve", lambda eng, T=T:
         eng.tensor_scalar_mul(T["thr_all"][:], T["thr_all_b"][:], 0.7 / D),
         waits=b0)
    thr_ok = p_bounce(P, T, T["thr_all"][:], T["thr_all_b"][:],
                      waits=[("ve", P.val("ve"))])
    for rs in range(NCH):
        g_v = p_dma(P, "gpsimd", "gp", T["tern_w"][:],
                    w_ap[rs * 128:(rs + 1) * 128, :],
                    waits=[("ve", P.val("ve")), ("ac", P.val("ac"))])
        p_op(P, "scalar", "ac", lambda eng, T=T:
             eng.activation(T["tern_abs"][:], T["tern_w"][:], AF.Abs),
             waits=[("gp", g_v)])
        p_op(P, "scalar", "ac", lambda eng, T=T:
             eng.activation(T["tern_sgn"][:], T["tern_w"][:], AF.Sign))
        a_pre = P.val("ac")
        p_op(P, "vector", "ve", lambda eng, T=T, rs=rs:
             eng.tensor_scalar(T["tern_ge"][:], T["tern_abs"][:],
                               T["thr_all_b"][:, rs:rs + 1], None, ALU.is_gt),
             waits=[("ac", a_pre)] + thr_ok)
        p_op(P, "vector", "ve", lambda eng, T=T:
             eng.tensor_tensor(T["tern_q"][:], T["tern_sgn"][:], T["tern_ge"][:],
                               ALU.mult))
        v_q = P.val("ve")
        p_dma(P, "sync", "sy", qf16_dram[rs * 128:(rs + 1) * 128, :], T["tern_q"][:],
              waits=[("ve", v_q)])
    sy_rows = P.val("sy")
    for cs in range(NCH):
        t_v = p_dma_tr(P, "sync", "sy", T["wtstage"][:, 0:D],
                       qf16_dram[0:D, cs * 128:(cs + 1) * 128],
                       waits=[("sy", sy_rows), ("gp", P.val("gp"))])
        p_dma(P, "gpsimd", "gp", qT_dram[cs * 128:(cs + 1) * 128, 0:D],
              T["wtstage"][:, 0:D], waits=[("sy", t_v)])
    return [("gp", P.val("gp"))]


def alloc_l0ffn(nc, ctx, T):
    t = lambda name, shape, dt: ctx.enter_context(nc.sbuf_tensor(name, shape, dt))
    if "arena" not in T:
        T["arena"] = t("arena", [128, 24576], mybir.dt.uint8)
    if "arena2" not in T:
        T["arena2"] = t("arena2", [128, 12288], mybir.dt.uint8)
    ar32 = T["arena"][:].bitcast(F32)
    T["tern_w"] = ar32[:, 0:D]
    T["tern_abs"] = ar32[:, D:2 * D]
    T["tern_sgn"] = ar32[:, 2 * D:3 * D]
    ar2 = T["arena2"][:].bitcast(F16)
    T["tern_ge"] = ar2[:, 0:D]
    T["tern_q"] = ar2[:, D:2 * D]
    T["bias_sb"] = t("bias_sb", [128, NCH], F32)


def emit_l0ffn(P, T, const_ok, w_ap, bias_ap, qf16_dram, qT_dram):
    b_v = p_dma(P, "gpsimd", "gp", T["bias_sb"][:], bias_ap)
    qT_ok = emit_l0ffn_ternary_prep(P, T, w_ap, qf16_dram, qT_dram)
    ln_ok = emit_ln_all(P, T, const_ok, T["stream"], T["xlnT"])
    return emit_proj_to_stream(P, T, qT_dram, qT_ok + ln_ok + [("gp", b_v)],
                               bias_col=T["bias_sb"])



def alloc_moe(nc, ctx, T):
    t = lambda name, shape, dt: ctx.enter_context(nc.sbuf_tensor(name, shape, dt))
    if "arena" not in T:
        T["arena"] = t("arena", [128, 24576], mybir.dt.uint8)
    if "arena2" not in T:
        T["arena2"] = t("arena2", [128, 12288], mybir.dt.uint8)
    arf = T["arena"][:].bitcast(F32)
    arh = T["arena"][:].bitcast(F16)
    T["gate_l"] = arf[:, 0:NTC * NE].rearrange("p (a b) -> p a b", b=NE)
    T["pgsel"] = arf[:, NTC * NE:2 * NTC * NE].rearrange("p (a b) -> p a b", b=NE)
    T["gmax"] = arf[:, 1024:1024 + NTC]
    T["negmax_b"] = arf[:, 1040:1040 + NTC]
    T["top8"] = arf[:, 1056:1056 + NTC * 8].rearrange("p (a b) -> p a b", b=8)
    T["top8_b"] = arf[:, 1184:1184 + NTC * 8].rearrange("p (a b) -> p a b", b=8)
    T["gsum"] = arf[:, 1312:1312 + NTC]
    T["grcp_b"] = arf[:, 1328:1328 + NTC]
    T["eprobs"] = arh[:, 2688:2688 + NTC * NE].rearrange("p (a b) -> p a b", b=NE)
    T["pg16"] = arh[:, 3200:3200 + NTC * NE].rearrange("p (a b) -> p a b", b=NE)
    ar2 = T["arena2"][:].bitcast(F16)
    T["pgT"] = ar2[0:32, 0:TOK]
    T["wrow16"] = ar2[0:1, TOK:2 * TOK]
    T["wbc"] = ar2[:, 2 * TOK:3 * TOK]
    T["mgT"] = t("mgT", [32, 1], F32)
    T["onehotT"] = t("onehotT", [32, G], F16)
    T["actbuf"] = T["arena"][:].bitcast(F16).rearrange("p (a b) -> p a b", b=TOK)
    T["gsil"] = t("gsil", [128, 512], F16)
    T["upw"] = t("upw", [128, 512], F16)
    T["dstage"] = t("dstage", [128, 512], F32)
    T["fin32"] = T["arena"][:].bitcast(F32)[:, 0:D]


def emit_l1ffn(P, T, const_ok, gw_ap, gu_ap, dn_ap, mgT_ap, dr, out_ap):
    gw_ok = emit_weight_prep_f32(P, T, gw_ap, NE, D, dr["gw16"], dr["gwT"], "gw")
    ln_ok = emit_ln_all(P, T, const_ok, T["stream"], T["xlnT"])
    mg_v = p_dma(P, "gpsimd", "gp", T["mgT"][:], mgT_ap)

    gt_v = p_dma(P, "sync", "sy", T["lhs_slice"][:, 0:NCH, 0:NE],
                 dr["gwT"].rearrange("(kc p) o -> p kc o", p=128),
                 waits=gw_ok + [("pe", P.val("pe"))])
    for tc in range(NTC):
        def mmg(eng, T=T, tc=tc):
            last = None
            for k in range(NCH):
                last = eng.matmul(T["ps_a"][:, 0:NE],
                                  T["xlnT"][:, k, tc * 128:(tc + 1) * 128],
                                  T["lhs_slice"][:, k, 0:NE],
                                  start=(k == 0), stop=(k == NCH - 1))
            return last
        p_op(P, "tensor", "pe", mmg,
             waits=ln_ok + [("sy", gt_v), ("ac", P.val("ac"))])
        pe_v = P.val("pe")
        p_op(P, "scalar", "ac", lambda eng, T=T, tc=tc:
             eng.activation(T["gate_l"][:, tc, :], T["ps_a"][:, 0:NE],
                            AF.Identity, scale=T["one_f32"][:]),
             waits=[("pe", pe_v)])
        ac_v = P.val("ac")
        p_op(P, "vector", "ve", lambda eng, T=T, tc=tc:
             eng.tensor_reduce(T["gmax"][:, tc:tc + 1], T["gate_l"][:, tc, :],
                               AX.X, ALU.max), waits=[("ac", ac_v)])
    b1 = p_bounce(P, T, T["gmax"][:], T["gmax"][:], waits=[("ve", P.val("ve"))])
    p_op(P, "vector", "ve", lambda eng, T=T:
         eng.tensor_scalar_mul(T["negmax_b"][:], T["gmax"][:], -1.0), waits=b1)
    b2 = p_bounce(P, T, T["negmax_b"][:], T["negmax_b"][:],
                  waits=[("ve", P.val("ve"))])
    for tc in range(NTC):
        p_op(P, "scalar", "ac", lambda eng, T=T, tc=tc:
             eng.activation(T["eprobs"][:, tc, :], T["gate_l"][:, tc, :], AF.Exp,
                            bias=T["negmax_b"][:, tc:tc + 1]), waits=b2)
    a_exp = P.val("ac")
    for tc in range(NTC):
        p_op(P, "vector", "ve", lambda eng, T=T, tc=tc:
             eng.max(T["top8"][:, tc, :], T["eprobs"][:, tc, :]),
             waits=[("ac", a_exp)])
    b3 = p_bounce(P, T, T["top8"][:].rearrange("p a b -> p (a b)"),
                  T["top8_b"][:].rearrange("p a b -> p (a b)"),
                  waits=[("ve", P.val("ve"))])
    for tc in range(NTC):
        p_op(P, "vector", "ve", lambda eng, T=T, tc=tc:
             eng.scalar_tensor_tensor(T["pgsel"][:, tc, :], T["eprobs"][:, tc, :],
                                      T["top8_b"][:, tc, 7:8],
                                      T["eprobs"][:, tc, :],
                                      ALU.is_ge, ALU.mult), waits=b3)
        p_op(P, "vector", "ve", lambda eng, T=T, tc=tc:
             eng.tensor_reduce(T["gsum"][:, tc:tc + 1], T["pgsel"][:, tc, :],
                               AX.X, ALU.add))
    b4 = p_bounce(P, T, T["gsum"][:], T["gsum"][:], waits=[("ve", P.val("ve"))])
    p_op(P, "vector", "ve", lambda eng, T=T:
         eng.reciprocal(T["grcp_b"][:], T["gsum"][:]), waits=b4)
    b5 = p_bounce(P, T, T["grcp_b"][:], T["grcp_b"][:],
                  waits=[("ve", P.val("ve"))])
    for tc in range(NTC):
        p_op(P, "vector", "ve", lambda eng, T=T, tc=tc:
             eng.tensor_scalar(T["pg16"][:, tc, :], T["pgsel"][:, tc, :],
                               T["grcp_b"][:, tc:tc + 1], None, ALU.mult),
             waits=b5)
        ve_v = P.val("ve")
        p_op(P, "tensor", "pe", lambda eng, T=T, tc=tc:
             eng.transpose(T["pt_a"][0:NE, 0:128], T["pg16"][:, tc, :],
                           T["ident"][:]),
             waits=[("ve", ve_v), ("ve", P.val("ve")), ("ac", P.val("ac"))])
        pe_v = P.val("pe")
        p_op(P, "scalar", "ac", lambda eng, T=T, tc=tc:
             eng.copy(T["pgT"][:, tc * 128:(tc + 1) * 128], T["pt_a"][0:NE, 0:128]),
             waits=[("pe", pe_v)])
    for g in range(G):
        p_op(P, "vector", "ve", lambda eng, T=T, g=g:
             eng.tensor_scalar(T["onehotT"][:, g:g + 1], T["mgT"][:], float(g),
                               None, ALU.is_equal),
             waits=[("gp", mg_v)] if g == 0 else ())
    oh_ok = p_bounce(P, T, T["onehotT"][:], T["onehotT"][:],
                     waits=[("ve", P.val("ve")), ("ac", P.val("ac"))])

    y3 = dr["y"].rearrange("(oc p) t -> p oc t", p=128)
    for g in range(G):
        gu_g = gu_ap.rearrange("(g r) c -> g r c", g=G)[g]
        dn_g = dn_ap.rearrange("(g r) c -> g r c", g=G)[g]
        gu_ok = emit_weight_prep_f32(P, T, gu_g, 2 * I_FF, D, dr["gu16"],
                                     dr["guT"], f"gu{g}")
        dn_ok = emit_weight_prep_f32(P, T, dn_g, D, I_FF, dr["dn16"],
                                     dr["dnT"], f"dn{g}")
        for qs in range(4):
            p_op(P, "tensor", "pe", lambda eng, T=T, g=g, qs=qs:
                 eng.matmul(T["ps_e"][0:1, :], T["onehotT"][:, g:g + 1],
                            T["pgT"][:, qs * 512:(qs + 1) * 512],
                            start=True, stop=True),
                 waits=oh_ok + [("ac", P.val("ac")), ("ve", P.val("ve"))])
            pe_v = P.val("pe")
            p_op(P, "scalar", "ac", lambda eng, T=T, qs=qs:
                 eng.copy(T["wrow16"][0:1, qs * 512:(qs + 1) * 512],
                          T["ps_e"][0:1, :]), waits=[("pe", pe_v)])
            ac_v = P.val("ac")
            p_op(P, "tensor", "pe", lambda eng, T=T, qs=qs:
                 eng.matmul(T["ps_f"][:], T["ones1"][:],
                            T["wrow16"][0:1, qs * 512:(qs + 1) * 512],
                            start=True, stop=True),
                 waits=[("ac", ac_v), ("ve", P.val("ve"))])
            pe_v2 = P.val("pe")
            p_op(P, "vector", "ve", lambda eng, T=T, qs=qs:
                 eng.tensor_copy(T["wbc"][:, qs * 512:(qs + 1) * 512],
                                 T["ps_f"][:]), waits=[("pe", pe_v2)])
        wbc_ok = [("ve", P.val("ve"))]
        for i in range(6):
            lg = dr["guT"].rearrange("(kc p) o -> p kc o", p=128)[:, :,
                                                                 i * 128:(i + 1) * 128]
            lu = dr["guT"].rearrange("(kc p) o -> p kc o", p=128)[:, :,
                                                                 (i + 6) * 128:(i + 7) * 128]
            g_v = p_dma(P, "sync", "sy", T["lhs_slice"][:, 0:NCH, :], lg,
                        waits=gu_ok + [("pe", P.val("pe"))])
            u_v = p_dma(P, "sync", "sy", T["lhs_slice2"][:, 0:NCH, :], lu,
                        waits=gu_ok + [("pe", P.val("pe"))])
            for qs in range(4):
                def mmgu(eng, T=T, qs=qs):
                    for k in range(NCH):
                        eng.matmul(T["ps_a"][:], T["lhs_slice"][:, k, :],
                                   T["xlnT"][:, k, qs * 512:(qs + 1) * 512],
                                   start=(k == 0), stop=(k == NCH - 1))
                    last = None
                    for k in range(NCH):
                        last = eng.matmul(T["ps_b"][:], T["lhs_slice2"][:, k, :],
                                          T["xlnT"][:, k, qs * 512:(qs + 1) * 512],
                                          start=(k == 0), stop=(k == NCH - 1))
                    return last
                p_op(P, "tensor", "pe", mmgu,
                     waits=[("sy", max(g_v, u_v)), ("ac", P.val("ac")),
                            ("ve", P.val("ve"))] + wbc_ok)
                pe_v = P.val("pe")
                p_op(P, "scalar", "ac", lambda eng, T=T:
                     eng.activation(T["gsil"][:], T["ps_a"][:], AF.Silu),
                     waits=[("pe", pe_v), ("ve", P.val("ve"))])
                ac_v = P.val("ac")
                p_op(P, "vector", "ve", lambda eng, T=T, qs=qs:
                     eng.tensor_tensor(T["upw"][:], T["ps_b"][:],
                                       T["wbc"][:, qs * 512:(qs + 1) * 512],
                                       ALU.mult), waits=[("pe", pe_v)])
                p_op(P, "vector", "ve", lambda eng, T=T, i=i, qs=qs:
                     eng.tensor_tensor(T["actbuf"][:, i, qs * 512:(qs + 1) * 512],
                                       T["gsil"][:], T["upw"][:], ALU.mult),
                     waits=[("ac", ac_v)])
        act_ok = [("ve", P.val("ve"))]
        for och in range(NCH):
            ld = dr["dnT"].rearrange("(kc p) o -> p kc o", p=128)[:, 0:6,
                                                                 och * 128:(och + 1) * 128]
            d_v = p_dma(P, "sync", "sy", T["lhs_slice"][:, 0:6, :], ld,
                        waits=dn_ok + [("pe", P.val("pe"))])
            for qs in range(4):
                def mmd(eng, T=T, qs=qs):
                    last = None
                    for k in range(6):
                        last = eng.matmul(T["ps_c"][:], T["lhs_slice"][:, k, :],
                                          T["actbuf"][:, k, qs * 512:(qs + 1) * 512],
                                          start=(k == 0), stop=(k == 5))
                    return last
                p_op(P, "tensor", "pe", mmd,
                     waits=[("sy", d_v), ("ve", P.val("ve"))] + act_ok)
                pe_v = P.val("pe")
                p_op(P, "vector", "ve", lambda eng, T=T:
                     eng.tensor_copy(T["dstage"][:], T["ps_c"][:]),
                     waits=[("pe", pe_v), ("gp", P.val("gp"))])
                ve_v = P.val("ve")
                dst = y3[:, och, qs * 512:(qs + 1) * 512]
                if g == 0:
                    p_dma(P, "gpsimd", "gp", dst, T["dstage"][:],
                          waits=[("ve", ve_v)])
                else:
                    p_dma(P, "gpsimd", "gp", dst, T["dstage"][:],
                          waits=[("ve", ve_v)], accum_op=ALU.add)
    y_ok = [("gp", P.val("gp"))]
    for tc in range(NTC):
        c_v = p_dma(P, "gpsimd", "gp", T["projsrc"][:, 0:NCH, 0:128],
                    dr["y"].rearrange("(oc p) t -> p oc t", p=128)[:, :,
                        tc * 128:(tc + 1) * 128],
                    waits=y_ok + [("pe", P.val("pe"))])
        def trs(eng, T=T):
            last = None
            for i in range(NCH):
                last = eng.transpose(
                    T["pt_b"][:, (i % 8) * 128:((i % 8) + 1) * 128]
                    if False else T["pt_b"][:, (i % 8) * 128:(i % 8 + 1) * 128],
                    T["projsrc"][:, i, 0:128], T["ident"][:])
            return last
        for half in range(2):
            def trs2(eng, T=T, half=half):
                last = None
                for i in range(8):
                    last = eng.transpose(T["pt_b"][:, i * 128:(i + 1) * 128],
                                         T["projsrc"][:, half * 8 + i, 0:128],
                                         T["ident"][:])
                return last
            p_op(P, "tensor", "pe", trs2,
                 waits=[("gp", c_v), ("ve", P.val("ve"))])
            pe_v = P.val("pe")
            def addf(eng, T=T, tc=tc, half=half):
                last = None
                for i in range(8):
                    oc = half * 8 + i
                    last = eng.tensor_tensor(
                        T["fin32"][:, oc * 128:(oc + 1) * 128],
                        T["pt_b"][:, i * 128:(i + 1) * 128],
                        T["stream"][:, tc, oc * 128:(oc + 1) * 128], ALU.add)
                return last
            p_op(P, "vector", "ve", addf, waits=[("pe", pe_v)])
        ve_v = P.val("ve")
        p_dma(P, "sync", "st", out_ap[tc * 128:(tc + 1) * 128, :], T["fin32"][:],
              waits=[("ve", ve_v)])
    fin = P.val("st")
    P.add("sync", lambda eng, sems, fin=fin: eng.wait_ge(sems["st"], fin))
    return [("st", fin)]


def build_l1ffn_launch():
    nc = mk_nc("l1ffn")
    x = nc.dram_tensor("x", [TOK, D], F32, kind="ExternalInput").ap()
    gw = nc.dram_tensor("gw", [NE, D], F32, kind="ExternalInput").ap()
    gu = nc.dram_tensor("gu", [G * 2 * I_FF, D], F32, kind="ExternalInput").ap()
    dn = nc.dram_tensor("dn", [G * D, I_FF], F32, kind="ExternalInput").ap()
    mgT = nc.dram_tensor("mgT_in", [32, 1], F32, kind="ExternalInput").ap()
    xout = nc.dram_tensor("xout", [TOK, D], F32, kind="ExternalOutput").ap()
    dr = {}
    for nm, shp in [("gw16", [NE, D]), ("gwT", [D, NE]),
                    ("gu16", [2 * I_FF, D]), ("guT", [D, 2 * I_FF]),
                    ("dn16", [D, I_FF]), ("dnT", [I_FF, D])]:
        dr[nm] = nc.dram_tensor(nm, shp, F16).ap()
    dr["y"] = nc.dram_tensor("y_acc", [D, TOK], F32).ap()
    T = {}
    ctx = ExitStack()
    with ctx:
        alloc_common(nc, ctx, T)
        alloc_moe(nc, ctx, T)
        sems = alloc_sems(nc, ctx, SEM_NAMES)
        block = ctx.enter_context(nc.Block())
        P = Plan()
        const_ok = emit_const_setup(P, T)
        st_ok = emit_stream_init(P, T, x)
        T["projsrc"] = T["xlnT"][:].rearrange("p c t -> p (c t)")[
            :, 20480:28672].rearrange("p (a t) -> p a t", t=512)
        emit_l1ffn(P, T, const_ok + st_ok, gw, gu, dn, mgT, dr, xout)
        P.emit(block, sems)
    return nc



def alloc_attn(nc, ctx, T):
    t = lambda name, shape, dt: ctx.enter_context(nc.sbuf_tensor(name, shape, dt))
    if "arena" not in T:
        T["arena"] = t("arena", [128, 24576], mybir.dt.uint8)
    ar16 = T["arena"][:].bitcast(F16)
    T["kbuf"] = ar16[:, 0:4096]
    T["vbuf"] = ar16[:, 4096:8192].rearrange("p (a b) -> p a b", b=128)
    T["qhead"] = ar16[:, 8192:10240]
    arh0 = T["arena"][:].bitcast(F16)
    T["ptile"] = arh0[:, 10240:11264].rearrange("p (a b) -> p a b", b=512)
    T["rrow"] = t("rrow", [1, 512], F32)
    T["rcb"] = t("rcb", [128, 512], F32)
    T["sqn16"] = arh0[:, 11264:11776]
    T["qn_all"] = t("qn_all", [1, 64], F32)
    T["kn_all"] = t("kn_all", [1, 64], F32)
    T["kn_all_b"] = t("kn_all_b", [1, 64], F32)
    T["sc1"] = t("sc1", [1, 8], F32)
    T["sc2"] = t("sc2", [1, 8], F32)
    T["sc16"] = t("sc16", [1, 8], F16)
    T["negMb"] = t("negMb", [128, 1], F32)
    T["amask"] = t("amask", [128, 4, 8, 128], F16)
    T["kown"] = t("kown", [128, 512], F16)
    T["kn_big"] = t("kn_big", [1, 128], F32)
    T["PTsb"] = t("PTsb", [RD, RD], F16)
    T["rope_st"] = None
    if "arena2" not in T:
        T["arena2"] = t("arena2", [128, 12288], mybir.dt.uint8)
    _ar2 = T["arena2"][:].bitcast(F16)
    T["cosq"] = _ar2[0:RD, 0:TOK]
    T["sinq"] = _ar2[0:RD, TOK:2 * TOK]
    T["rope_st"] = T["arena"][:].bitcast(F16)[0:RD, 5120:5632]
    T["rope_t1"] = arh0[0:RD, 4096:4608]
    T["rope_t2"] = arh0[0:RD, 4608:5120]
    flat = T["xlnT"][:].rearrange("p c t -> p (c t)")
    T["projsrc"] = flat[:, 20480:28672].rearrange("p (a t) -> p a t", t=512)
    T["attnstg"] = flat[:, 28672:30720]
    T["ckv_view"] = flat[:, 0:4 * S].rearrange("p (a t) -> p a t", t=S)
    T["kr_view"] = flat[0:RD, 4 * S:5 * S]
    T["sc_kn2"] = t("sc_kn2", [1, 64], F32)
    T["sc_kn3"] = t("sc_kn3", [1, 64], F32)
    T["scal_dram"] = nc.dram_tensor("scal_dram", [4, 8], F32)


def emit_sq_norm_col(P, T, src_f16_ap, dst_col_ap, pe_wait):
    p_op(P, "scalar", "ac", lambda eng, a=src_f16_ap, T=T:
         eng.activation(T["sqn16"][:], a, AF.Square),
         waits=list(pe_wait) + [("pe", P.val("pe"))])
    ac_v = P.val("ac")
    p_op(P, "tensor", "pe", lambda eng, T=T:
         eng.matmul(T["ps_e"][0:1, :], T["ones128"][:], T["sqn16"][:],
                    start=True, stop=True), waits=[("ac", ac_v)])
    pe_v = P.val("pe")
    p_op(P, "vector", "ve", lambda eng, T=T, d=dst_col_ap:
         eng.tensor_reduce(d, T["ps_e"][0:1, :], AX.X, ALU.max),
         waits=[("pe", pe_v)])
    return [("ve", P.val("ve"))]


def emit_negM(P, T, qn_ok, kn_ok, scal_dram):
    sd = scal_dram.ap()
    p_op(P, "vector", "ve", lambda eng, T=T:
         eng.tensor_reduce(T["sc1"][0:1, 0:1], T["qn_all"][:], AX.X, ALU.max),
         waits=qn_ok)
    p_op(P, "vector", "ve", lambda eng, T=T:
         eng.tensor_reduce(T["sc1"][0:1, 1:2], T["kn_all_b"][:], AX.X, ALU.max),
         waits=kn_ok)
    b1 = p_bounce(P, T, T["sc1"][0:1, 0:8], T["sc2"][0:1, 0:8],
                  waits=[("ve", P.val("ve"))])
    p_op(P, "vector", "ve", lambda eng, T=T:
         eng.tensor_tensor(T["sc1"][0:1, 2:3], T["sc2"][0:1, 0:1],
                           T["sc2"][0:1, 1:2], ALU.mult), waits=b1)
    b2 = p_bounce(P, T, T["sc1"][0:1, 0:8], T["sc2"][0:1, 0:8],
                  waits=[("ve", P.val("ve"))])
    p_op(P, "scalar", "ac", lambda eng, T=T:
         eng.activation(T["sc1"][0:1, 3:4], T["sc2"][0:1, 2:3], AF.Sqrt), waits=b2)
    b3 = p_bounce(P, T, T["sc1"][0:1, 0:8], T["sc2"][0:1, 0:8],
                  waits=[("ac", P.val("ac"))])
    p_op(P, "vector", "ve", lambda eng, T=T:
         eng.tensor_scalar(T["sc1"][0:1, 4:5], T["sc2"][0:1, 3:4], -1.0,
                           EXP_SHIFT, ALU.mult, ALU.add), waits=b3)
    v = p_dma(P, "sync", "sy", sd[0:1, 0:1], T["sc1"][0:1, 4:5],
              waits=[("ve", P.val("ve"))])
    bca = sd[0:1, 0:1].broadcast_to([128, 1])
    v2 = p_dma(P, "sync", "sy", T["negMb"][:], bca, waits=[("sy", v)])
    return [("sy", v2)]


def emit_attention_head(P, T, h, q_src_dram, attn_dst, negM_ok, mask_ok,
                        kq_wait=()):
    q_v = p_dma(P, "sync", "sy", T["qhead"][:], q_src_dram,
                waits=[("pe", P.val("pe"))])
    for qs in range(4):
        trips = trips_for_super(qs)
        for kc in range(trips):
            jm = jmin_for(qs, kc)
            coff = jm * 128
            def qk(eng, T=T, kc=kc, qs=qs, coff=coff):
                return eng.matmul(T["ps_b"][:, coff:512],
                                  T["kbuf"][:, kc * 128:(kc + 1) * 128],
                                  T["qhead"][:, qs * 512 + coff:(qs + 1) * 512],
                                  start=True, stop=True)
            w = [("sy", q_v), ("ac", P.val("ac"))] + list(kq_wait)
            if kc == 0:
                w += [("ve", P.val("ve"))]
            p_op(P, "tensor", "pe", qk, waits=w)
            pe_qk = P.val("pe")
            buf = kc % 2
            def ex(eng, T=T, buf=buf, coff=coff):
                return eng.activation(T["ptile"][:, buf, coff:512],
                                      T["ps_b"][:, coff:512], AF.Exp,
                                      bias=T["negMb"][:])
            p_op(P, "scalar", "ac", ex,
                 waits=[("pe", pe_qk)] + (negM_ok if (qs == 0 and kc == 0) else []))
            ac_ex = P.val("ac")
            last = [("ac", ac_ex)]
            if kc >= 8 * qs:
                e = kc - 8 * qs
                def mk(eng, T=T, buf=buf, coff=coff, qs=qs, e=e):
                    return eng.tensor_tensor(T["ptile"][:, buf, coff:coff + 128],
                                             T["ptile"][:, buf, coff:coff + 128],
                                             T["amask"][:, qs, e, :], ALU.mult)
                p_op(P, "vector", "ve", mk, waits=last + list(mask_ok))
                last = [("ac", ac_ex), ("ve", P.val("ve"))]
            def spv(eng, T=T, buf=buf, coff=coff, kc=kc, trips=trips):
                eng.matmul(T["ps_e"][0:1, coff:512], T["ones128"][:],
                           T["ptile"][:, buf, coff:512],
                           start=(kc == 0), stop=(kc == trips - 1))
                return eng.matmul(T["ps_c"][:, coff:512], T["vbuf"][:, kc, :],
                                  T["ptile"][:, buf, coff:512],
                                  start=(kc == 0), stop=(kc == trips - 1))
            p_op(P, "tensor", "pe", spv, waits=last)
        pe_fin = P.val("pe")
        p_op(P, "vector", "ve", lambda eng, T=T:
             eng.reciprocal(T["rrow"][:], T["ps_e"][0:1, :]), waits=[("pe", pe_fin)])
        p_op(P, "tensor", "pe", lambda eng, T=T:
             eng.matmul(T["ps_f"][:], T["ones1f32"][:], T["rrow"][:],
                        start=True, stop=True),
             waits=[("ve", P.val("ve")), ("ac", P.val("ac"))])
        p_op(P, "scalar", "ac", lambda eng, T=T:
             eng.activation(T["rcb"][:], T["ps_f"][:], AF.Identity,
                            scale=T["one_f32"][:]), waits=[("pe", P.val("pe"))])
        dst = attn_dst(qs)
        p_op(P, "vector", "ve", lambda eng, T=T, dst=dst:
             eng.tensor_tensor(dst, T["ps_c"][:], T["rcb"][:], ALU.mult),
             waits=[("ac", P.val("ac"))])
    return [("ve", P.val("ve"))]


def emit_proj_from_dram(P, T, wT_dram, src_dram, w_ready, src_ready):
    src3 = src_dram.rearrange("(kc p) t -> p kc t", p=128)
    for qs in range(4):
        s_v = p_dma(P, "sync", "sy", T["projsrc"][:],
                    src3[:, :, qs * 512:(qs + 1) * 512],
                    waits=src_ready + [("pe", P.val("pe"))])
        for och in range(NCH):
            lw = wT_dram.rearrange("(kc p) o -> p kc o", p=128)[:, :,
                                                               och * 128:(och + 1) * 128]
            ld_v = p_dma(P, "sync", "sy", T["lhs_slice"][:, 0:NCH, :], lw,
                         waits=w_ready + [("pe", P.val("pe"))])
            def mm(eng, T=T):
                last = None
                for k in range(NCH):
                    last = eng.matmul(T["ps_a"][:], T["lhs_slice"][:, k, :],
                                      T["projsrc"][:, k, :],
                                      start=(k == 0), stop=(k == NCH - 1))
                return last
            p_op(P, "tensor", "pe", mm,
                 waits=[("sy", max(s_v, ld_v)), ("ac", P.val("ac"))])
            pe_mm = P.val("pe")
            p_op(P, "scalar", "ac", lambda eng, T=T:
                 eng.activation(T["evac16"][:], T["ps_a"][:], AF.Identity,
                                scale=T["one_f32"][:]), waits=[("pe", pe_mm)])
            ac_v = P.val("ac")
            def trs(eng, T=T):
                last = None
                for i in range(4):
                    last = eng.transpose(T["pt_b"][:, i * 128:(i + 1) * 128],
                                         T["evac16"][:, i * 128:(i + 1) * 128],
                                         T["ident"][:])
                return last
            p_op(P, "tensor", "pe", trs, waits=[("ac", ac_v), ("ve", P.val("ve"))])
            pe_tr = P.val("pe")
            def add(eng, T=T, qs=qs, och=och):
                last = None
                for i in range(4):
                    tcc = qs * 4 + i
                    last = eng.tensor_tensor(
                        T["stream"][:, tcc, och * 128:(och + 1) * 128],
                        T["pt_b"][:, i * 128:(i + 1) * 128],
                        T["stream"][:, tcc, och * 128:(och + 1) * 128], ALU.add)
                return last
            p_op(P, "vector", "ve", add, waits=[("pe", pe_tr)])
    return [("ve", P.val("ve"))]



def emit_rope_rows_half(P, T, tile_T, psrot, cos_sl, sin_sl, evac_ac,
                        src_row0=0):
    if src_row0 == 0:
        rows = tile_T[0:64, :]
    else:
        p_op(P, "scalar", "ac", lambda eng, T=T, t=tile_T, s0=src_row0:
             eng.copy(T["rope_st"][:], t[s0:s0 + 64, :]), waits=[("ac", evac_ac)])
        evac_ac = P.val("ac")
        rows = T["rope_st"][:]
    p_op(P, "tensor", "pe", lambda eng, T=T, r=rows, ps=psrot:
         eng.matmul(ps, T["PTsb"][:], r, start=True, stop=True),
         waits=[("ac", evac_ac), ("ve", P.val("ve"))])
    pe_rot = P.val("pe")
    p_op(P, "vector", "ve", lambda eng, T=T, r=rows, c=cos_sl:
         eng.tensor_tensor(T["rope_t1"][:], r, c, ALU.mult),
         waits=[("ac", evac_ac), ("pe", pe_rot)])
    p_op(P, "vector", "ve", lambda eng, T=T, ps=psrot, s=sin_sl:
         eng.tensor_tensor(T["rope_t2"][:], ps, s, ALU.mult))
    dst = tile_T[src_row0:src_row0 + 64, :]
    p_op(P, "vector", "ve", lambda eng, T=T, d=dst:
         eng.tensor_tensor(d, T["rope_t1"][:], T["rope_t2"][:], ALU.add))
    return P.val("ve")


def emit_l0attn(P, T, const_ok, ex, wq_ap, wkvd_ap, wkup_ap, wkrope_ap,
                wvup_ap, wo_ap, dr):
    wq_ok = emit_weight_prep_f32(P, T, wq_ap, D, D, dr["wq16"], dr["wqT"], "wq0")
    wkvd_ok = emit_weight_prep_f32(P, T, wkvd_ap, R_KV, D, dr["wkvd16"],
                                   dr["wkvdT"], "wkvd")
    wkup_ok = emit_weight_prep_f32(P, T, wkup_ap, H * ND, R_KV, dr["wkup16"],
                                   dr["wkupT"], "wkup")
    wkr_ok = emit_weight_prep_f32(P, T, wkrope_ap, RD, D, dr["wkr16"],
                                  dr["wkrT"], "wkr")
    wvup_ok = emit_weight_prep_f32(P, T, wvup_ap, D, R_KV, dr["wvup16"],
                                   dr["wvupT"], "wvup")
    wo_ok = emit_weight_prep_f32(P, T, wo_ap, D, D, dr["wo16"], dr["woT"], "wo0")
    ln_ok = emit_ln_all(P, T, const_ok, T["stream"], T["xlnT"])
    p_op(P, "vector", "ve", lambda eng, T=T: eng.memset(T["qn_all"][:], 0.0),
         waits=const_ok)
    p_op(P, "vector", "ve", lambda eng, T=T: eng.memset(T["kn_big"][:], 0.0))

    sh_ckv = ex["ckv"].ap().rearrange("(k p) (j two c) -> k p j two c",
                                      p=128, two=2, c=128)
    sh_kr = ex["kr"].ap().rearrange("p (j two c) -> p j two c", two=2, c=128)
    for rc in range(4):
        lw = dr["wkvdT"].rearrange("(kc p) o -> p kc o", p=128)[:, :,
                                                               rc * 128:(rc + 1) * 128]
        l_v = p_dma(P, "sync", "sy", T["lhs_slice"][:, 0:NCH, :], lw,
                    waits=wkvd_ok + ln_ok + [("pe", P.val("pe"))])
        for qs in range(4):
            def mmc(eng, T=T, qs=qs):
                last = None
                for k in range(NCH):
                    last = eng.matmul(T["ps_a"][:], T["lhs_slice"][:, k, :],
                                      T["xlnT"][:, k, qs * 512:(qs + 1) * 512],
                                      start=(k == 0), stop=(k == NCH - 1))
                return last
            p_op(P, "tensor", "pe", mmc, waits=[("sy", l_v), ("ac", P.val("ac"))])
            pe_v = P.val("pe")
            p_op(P, "scalar", "ac", lambda eng, T=T:
                 eng.activation(T["kown"][:], T["ps_a"][:], AF.Identity,
                                scale=T["one_f32"][:]),
                 waits=[("pe", pe_v), ("gx", P.val("gx")), ("sy", P.val("sy"))])
            ac_v = P.val("ac")
            src = T["kown"][:].rearrange("p (j c) -> p j c", c=128)
            de = sh_ckv[rc, :, 4 * qs:4 * qs + 4, 0, :]
            do = sh_ckv[rc, :, 4 * qs:4 * qs + 4, 1, :]
            p_shared_write(P, T, de, do, src, waits=[("ac", ac_v)])
    lkr = dr["wkrT"].rearrange("(kc p) o -> p kc o", p=128)
    kr_v = p_dma(P, "sync", "sy", T["lhs_slice2"][:, 0:NCH, 0:RD], lkr,
                 waits=wkr_ok + ln_ok + [("pe", P.val("pe"))])
    for qs in range(4):
        def mmr(eng, T=T, qs=qs):
            last = None
            for k in range(NCH):
                last = eng.matmul(T["ps_d"][0:RD, :], T["lhs_slice2"][:, k, 0:RD],
                                  T["xlnT"][:, k, qs * 512:(qs + 1) * 512],
                                  start=(k == 0), stop=(k == NCH - 1))
            return last
        p_op(P, "tensor", "pe", mmr, waits=[("sy", kr_v), ("ac", P.val("ac"))])
        pe_v = P.val("pe")
        p_op(P, "scalar", "ac", lambda eng, T=T:
             eng.activation(T["kown"][0:RD, :], T["ps_d"][0:RD, :], AF.Identity,
                            scale=T["one_f32"][0:RD, :]),
             waits=[("pe", pe_v), ("gx", P.val("gx")), ("ve", P.val("ve"))])
        ac_v = P.val("ac")
        emit_rope_rows_half(P, T, T["kown"], T["ps_b"][0:RD, :],
                            T["cosq"][:, qs * 512:(qs + 1) * 512],
                            T["sinq"][:, qs * 512:(qs + 1) * 512], ac_v)
        ve_v = P.val("ve")
        src = T["kown"][0:RD, :].rearrange("p (j c) -> p j c", c=128)
        de = sh_kr[0:RD, 4 * qs:4 * qs + 4, 0, :]
        do = sh_kr[0:RD, 4 * qs:4 * qs + 4, 1, :]
        p_shared_write(P, T, de, do, src, waits=[("ve", ve_v)])
    kv_writes_done = [("gx", P.val("gx"))]
    if L0_STAGE <= 1:
        return kv_writes_done

    for h in range(H):
        lq = dr["wqT"].rearrange("(kc p) o -> p kc o", p=128)[:, :,
                                                             h * 128:(h + 1) * 128]
        q_v = p_dma(P, "sync", "sy", T["lhs_slice"][:, 0:NCH, :], lq,
                    waits=wq_ok + ln_ok + [("pe", P.val("pe"))])
        for qs in range(4):
            def mmq(eng, T=T, qs=qs):
                last = None
                for k in range(NCH):
                    last = eng.matmul(T["ps_a"][:], T["lhs_slice"][:, k, :],
                                      T["xlnT"][:, k, qs * 512:(qs + 1) * 512],
                                      start=(k == 0), stop=(k == NCH - 1))
                return last
            p_op(P, "tensor", "pe", mmq, waits=[("sy", q_v), ("ac", P.val("ac"))])
            pe_v = P.val("pe")
            p_op(P, "scalar", "ac", lambda eng, T=T:
                 eng.activation(T["kown"][:], T["ps_a"][:], AF.Identity,
                                scale=T["scale_qk"][:]),
                 waits=[("pe", pe_v), ("sy", P.val("sy")), ("ve", P.val("ve")),
                        ("gx", P.val("gx"))])
            ac_v = P.val("ac")
            emit_rope_rows_half(P, T, T["kown"], T["ps_b"][0:RD, :],
                                T["cosq"][:, qs * 512:(qs + 1) * 512],
                                T["sinq"][:, qs * 512:(qs + 1) * 512], ac_v,
                                src_row0=64)
            ve_v = P.val("ve")
            nidx = h * 4 + qs
            emit_sq_norm_col(P, T, T["kown"][:], T["qn_all"][0:1, nidx:nidx + 1],
                             [("ac", ac_v), ("ve", ve_v)])
            p_dma(P, "sync", "sy",
                  dr["qspill"].rearrange("(hh p) t -> hh p t", p=128)[h][:,
                      qs * 512:(qs + 1) * 512],
                  T["kown"][:], waits=[("ve", ve_v), ("ve", P.val("ve"))])
    q_done = [("sy", P.val("sy")), ("ve", P.val("ve"))]

    sp_ok = emit_flag_and_spin(P, T, ex["idx"], kv_writes_done)
    ckv_view = T["ckv_view"]
    kr_view = T["kr_view"]
    rb1 = p_dma(P, "sync", "sy", ckv_view,
                ex["ckv"].ap().rearrange("(k p) t -> p k t", p=128),
                waits=sp_ok + q_done + [("pe", P.val("pe"))])
    rb2 = p_dma(P, "sync", "sy", kr_view, ex["kr"].ap(), waits=sp_ok)
    rb = [("sy", max(rb1, rb2))]
    if L0_STAGE <= 2:
        return rb

    for h in range(H):
        lku = dr["wkupT"].rearrange("(kc p) o -> p kc o", p=128)[:, 0:4,
                                                                h * ND:(h + 1) * ND]
        ku_v = p_dma(P, "sync", "sy", T["lhs_slice2"][:, 0:4, 0:ND], lku,
                     waits=wkup_ok + rb + [("pe", P.val("pe"))])
        for nt in range(8):
            def mmk(eng, T=T, nt=nt):
                last = None
                for rc in range(4):
                    last = eng.matmul(T["ps_d"][0:ND, :],
                                      T["lhs_slice2"][:, rc, 0:ND],
                                      T["ckv_view"][:, rc,
                                                    nt * 512:(nt + 1) * 512],
                                      start=(rc == 0), stop=(rc == 3))
                return last
            p_op(P, "tensor", "pe", mmk, waits=[("sy", ku_v), ("ac", P.val("ac"))])
            pe_v = P.val("pe")
            p_op(P, "scalar", "ac", lambda eng, T=T, nt=nt:
                 eng.activation(T["kbuf"][0:ND, nt * 512:(nt + 1) * 512],
                                T["ps_d"][0:ND, :], AF.Identity,
                                scale=T["one_f32"][0:ND, :]),
                 waits=[("pe", pe_v), ("sy", P.val("sy"))])
        ac_k = P.val("ac")
        kr_c = P.val("sy")
        if L0_SUB >= 2:
            kr_c = p_dma(P, "sync", "sy", T["kbuf"][64:128, :], T["kr_view"],
                         waits=[("ac", ac_k), ("sy", rb2)])
        ve_k = P.val("ve")
        if L0_SUB >= 3:
            for nt in range(8):
                emit_sq_norm_col(P, T, T["kbuf"][:, nt * 512:(nt + 1) * 512],
                                 T["kn_big"][0:1, h * 8 + nt:h * 8 + nt + 1],
                                 [("ac", ac_k), ("ve", ve_k), ("sy", kr_c)])
        if L0_SUB >= 4:
            p_dma(P, "sync", "sy",
                  dr["kspill"].rearrange("(hh p) t -> hh p t", p=128)[h],
                  T["kbuf"][:], waits=[("ve", P.val("ve")), ("ac", P.val("ac"))])
    k_done = [("sy", P.val("sy"))]
    if L0_STAGE <= 3:
        return k_done

    qn_b = p_bounce(P, T, T["qn_all"][:], T["qn_all"][:], waits=q_done)
    kn_b = p_bounce(P, T, T["kn_big"][:], T["kn_big"][:],
                    waits=[("ve", P.val("ve"))])
    p_op(P, "vector", "ve", lambda eng, T=T:
         eng.tensor_tensor(T["kn_all_b"][0:1, :], T["kn_big"][0:1, 0:64],
                           T["kn_big"][0:1, 64:128], ALU.max), waits=kn_b)
    kn_ok = p_bounce(P, T, T["kn_all_b"][:], T["kn_all_b"][:],
                     waits=[("ve", P.val("ve"))])
    negM_ok = emit_negM(P, T, qn_b, kn_ok, T["scal_dram"])

    mask_ok = [("gp", P.val("gp"))]
    if L0_STAGE <= 4:
        return negM_ok
    att3 = dr["attnspill"].rearrange("(hh p) t -> hh p t", p=128)
    for h in range(H):
        kb_v = p_dma(P, "sync", "sy", T["kbuf"][:],
                     dr["kspill"].rearrange("(hh p) t -> hh p t", p=128)[h],
                     waits=k_done + [("pe", P.val("pe"))])
        lvu = dr["wvupT"].rearrange("(kc p) o -> p kc o", p=128)[:, 0:4,
                                                                h * 128:(h + 1) * 128]
        vu_v = p_dma(P, "sync", "sy", T["lhs_slice2"][:, 0:4, :], lvu,
                     waits=wvup_ok + [("pe", P.val("pe"))])
        for vt in range(8):
            def mmv(eng, T=T, vt=vt):
                last = None
                for j in range(4):
                    kc = vt * 4 + j
                    for rc in range(4):
                        last = eng.matmul(
                            T["ps_a"][:, j * 128:(j + 1) * 128],
                            T["ckv_view"][:, rc, kc * 128:(kc + 1) * 128],
                            T["lhs_slice2"][:, rc, :],
                            start=(rc == 0), stop=(rc == 3))
                return last
            p_op(P, "tensor", "pe", mmv, waits=[("sy", vu_v), ("ac", P.val("ac"))])
            pe_v = P.val("pe")
            p_op(P, "scalar", "ac", lambda eng, T=T, vt=vt:
                 eng.copy(T["vbuf"][:, vt * 4:(vt + 1) * 4, :]
                          .rearrange("p a b -> p (a b)"),
                          T["ps_a"][:]), waits=[("pe", pe_v)])
        attn_dst = (lambda qs, h=h: T["attnstg"][:, qs * 512:(qs + 1) * 512])
        emit_attention_head(P, T, h, dr["qspill"]
                            .rearrange("(hh p) t -> hh p t", p=128)[h],
                            attn_dst, negM_ok, mask_ok, kq_wait=[("sy", kb_v)])
        p_dma(P, "sync", "sy", att3[h], T["attnstg"][:],
              waits=[("ve", P.val("ve"))])
    attn_done = [("sy", P.val("sy"))]
    if L0_STAGE <= 5:
        return attn_done
    return emit_proj_from_dram(P, T, dr["woT"], dr["attnspill"], wo_ok, attn_done)



def emit_l1attn(P, T, const_ok, ex, wq_ap, wk_ap, wv_ap, wo_ap,
                wqT_d, wkT_d, wvT_d, woT_d, qspill_d):
    wq_ok = emit_weight_prep_f16(P, T, wq_ap, D, D, wqT_d, "wq1")
    wk_ok = emit_weight_prep_f16(P, T, wk_ap, 512, D, wkT_d, "wk1")
    wv_ok = emit_weight_prep_f16(P, T, wv_ap, 512, D, wvT_d, "wv1")
    wo_ok = emit_weight_prep_f16(P, T, wo_ap, D, D, woT_d, "wo1")
    ln_ok = emit_ln_all(P, T, const_ok, T["stream"], T["xlnT"])
    p_op(P, "vector", "ve", lambda eng, T=T: eng.memset(T["qn_all"][:], 0.0),
         waits=const_ok)
    p_op(P, "vector", "ve", lambda eng, T=T: eng.memset(T["kn_all"][:], 0.0))

    sh_k = ex["k"].ap().rearrange("(k p) (j two c) -> k p j two c",
                                  p=128, two=2, c=128)
    sh_v = ex["v"].ap().rearrange("(k j two p) c -> k p j two c",
                                  p=128, two=2, j=NTC)
    for kvh in range(H_KV):
        lwk = wkT_d.rearrange("(kc p) o -> p kc o", p=128)[:, :,
                                                           kvh * 128:(kvh + 1) * 128]
        k_v = p_dma(P, "sync", "sy", T["lhs_slice"][:, 0:NCH, :], lwk,
                    waits=wk_ok + ln_ok + [("pe", P.val("pe"))])
        for qs in range(4):
            def mmk(eng, T=T, qs=qs):
                last = None
                for k in range(NCH):
                    last = eng.matmul(T["ps_a"][:], T["lhs_slice"][:, k, :],
                                      T["xlnT"][:, k, qs * 512:(qs + 1) * 512],
                                      start=(k == 0), stop=(k == NCH - 1))
                return last
            p_op(P, "tensor", "pe", mmk, waits=[("sy", k_v), ("ac", P.val("ac"))])
            pe_v = P.val("pe")
            p_op(P, "scalar", "ac", lambda eng, T=T:
                 eng.activation(T["kown"][:], T["ps_a"][:], AF.Identity,
                                scale=T["one_f32"][:]),
                 waits=[("pe", pe_v), ("gx", P.val("gx")), ("sy", P.val("sy"))])
            ac_v = P.val("ac")
            nidx = kvh * 4 + qs
            emit_sq_norm_col(P, T, T["kown"][:], T["kn_all"][0:1, nidx:nidx + 1],
                             [("ac", ac_v)])
            src = T["kown"][:].rearrange("p (j c) -> p j c", c=128)
            de = sh_k[kvh, :, 4 * qs:4 * qs + 4, 0, :]
            do = sh_k[kvh, :, 4 * qs:4 * qs + 4, 1, :]
            p_shared_write(P, T, de, do, src, waits=[("ac", ac_v)])
        lwv = wvT_d.rearrange("(kc p) o -> p kc o", p=128)[:, :,
                                                           kvh * 128:(kvh + 1) * 128]
        v_v = p_dma(P, "sync", "sy", T["lhs_slice2"][:, 0:NCH, :], lwv,
                    waits=wv_ok + ln_ok + [("pe", P.val("pe"))])
        for qs in range(4):
            def mmv(eng, T=T, qs=qs):
                last = None
                for j in range(4):
                    sl = qs * 4 + j
                    for k in range(NCH):
                        last = eng.matmul(
                            T["ps_a"][:, j * 128:(j + 1) * 128],
                            T["xlnT"][:, k, sl * 128:(sl + 1) * 128],
                            T["lhs_slice2"][:, k, :],
                            start=(k == 0), stop=(k == NCH - 1))
                return last
            p_op(P, "tensor", "pe", mmv, waits=[("sy", v_v), ("ac", P.val("ac"))])
            pe_v = P.val("pe")
            p_op(P, "scalar", "ac", lambda eng, T=T:
                 eng.activation(T["kown"][:], T["ps_a"][:], AF.Identity,
                                scale=T["one_f32"][:]),
                 waits=[("pe", pe_v), ("gx", P.val("gx")), ("sy", P.val("sy"))])
            ac_v = P.val("ac")
            src = T["kown"][:].rearrange("p (j c) -> p j c", c=128)
            de = sh_v[kvh, :, 4 * qs:4 * qs + 4, 0, :]
            do = sh_v[kvh, :, 4 * qs:4 * qs + 4, 1, :]
            p_shared_write(P, T, de, do, src, waits=[("ac", ac_v)])
    sh_kn = ex["kn"].ap()
    v_kn = P.val("ve")
    p_shared_write(P, T, sh_kn[0:1, :], sh_kn[1:2, :], T["kn_all"][:],
                   waits=[("ve", v_kn)])
    kv_writes_done = [("gx", P.val("gx"))]

    for h in range(H):
        lq = wqT_d.rearrange("(kc p) o -> p kc o", p=128)[:, :,
                                                          h * 128:(h + 1) * 128]
        q_v = p_dma(P, "sync", "sy", T["lhs_slice"][:, 0:NCH, :], lq,
                    waits=wq_ok + ln_ok + [("pe", P.val("pe"))])
        for qs in range(4):
            def mmq(eng, T=T, qs=qs):
                last = None
                for k in range(NCH):
                    last = eng.matmul(T["ps_a"][:], T["lhs_slice"][:, k, :],
                                      T["xlnT"][:, k, qs * 512:(qs + 1) * 512],
                                      start=(k == 0), stop=(k == NCH - 1))
                return last
            p_op(P, "tensor", "pe", mmq, waits=[("sy", q_v), ("ac", P.val("ac"))])
            pe_v = P.val("pe")
            p_op(P, "scalar", "ac", lambda eng, T=T:
                 eng.activation(T["kown"][:], T["ps_a"][:], AF.Identity,
                                scale=T["scale_qk"][:]),
                 waits=[("pe", pe_v), ("sy", P.val("sy")), ("gx", P.val("gx"))])
            ac_v = P.val("ac")
            nidx = h * 4 + qs
            emit_sq_norm_col(P, T, T["kown"][:], T["qn_all"][0:1, nidx:nidx + 1],
                             [("ac", ac_v)])
            p_dma(P, "sync", "sy",
                  qspill_d[h * 128:(h + 1) * 128, qs * 512:(qs + 1) * 512],
                  T["kown"][:], waits=[("ac", ac_v)])
    q_done = [("sy", P.val("sy")), ("ve", P.val("ve"))]

    sp_ok = emit_flag_and_spin(P, T, ex["idx"], kv_writes_done)

    p_dma(P, "gpsimd", "gp", T["sc_kn2"][:], ex["kn"].ap()[0:1, :], waits=sp_ok)
    rd_kn = p_dma(P, "gpsimd", "gp", T["sc_kn3"][:], ex["kn"].ap()[1:2, :],
                  waits=sp_ok)
    p_op(P, "vector", "ve", lambda eng, T=T:
         eng.tensor_tensor(T["kn_all_b"][0:1, :], T["sc_kn2"][0:1, :],
                           T["sc_kn3"][0:1, :], ALU.max), waits=[("gp", rd_kn)])
    kn_ok = p_bounce(P, T, T["kn_all_b"][:], T["kn_all_b"][:],
                     waits=[("ve", P.val("ve"))])
    qn_b = p_bounce(P, T, T["qn_all"][:], T["qn_all"][:], waits=q_done)
    negM_ok = emit_negM(P, T, qn_b, kn_ok, T["scal_dram"])

    mask_v = P.val("gp")
    for kvh in range(H_KV):
        kb_v = p_dma(P, "sync", "sy", T["kbuf"][:],
                     ex["k"].ap().rearrange("(k p) t -> k p t", p=128)[kvh],
                     waits=sp_ok + [("pe", P.val("pe"))])
        vb_v = p_dma(P, "sync", "sy", T["vbuf"][:],
                     ex["v"].ap().rearrange("(k kc p) c -> k p kc c",
                                            p=128, kc=32)[kvh],
                     waits=sp_ok + [("pe", P.val("pe"))])
        for hq in range(4):
            h = kvh * 4 + hq
            attn_dst = (lambda qs, h=h:
                        T["xlnT"][:, h, qs * 512:(qs + 1) * 512])
            emit_attention_head(P, T, h,
                                qspill_d[h * 128:(h + 1) * 128, :],
                                attn_dst, negM_ok, [("gp", mask_v)],
                                kq_wait=[("sy", max(kb_v, vb_v))])
    return emit_proj_to_stream(P, T, woT_d, wo_ok + [("ve", P.val("ve"))])




def build_l0attn_launch():
    nc = mk_nc("l0attn")
    x = nc.dram_tensor("x", [TOK, D], F32, kind="ExternalInput").ap()
    wq = nc.dram_tensor("wq", [D, D], F32, kind="ExternalInput").ap()
    wkvd = nc.dram_tensor("wkvd", [R_KV, D], F32, kind="ExternalInput").ap()
    wkup = nc.dram_tensor("wkup", [H * ND, R_KV], F32, kind="ExternalInput").ap()
    wkrope = nc.dram_tensor("wkrope", [RD, D], F32, kind="ExternalInput").ap()
    wvup = nc.dram_tensor("wvup", [D, R_KV], F32, kind="ExternalInput").ap()
    wo = nc.dram_tensor("wo", [D, D], F32, kind="ExternalInput").ap()
    amask_in = nc.dram_tensor("amask_in", [128, 4 * 8 * 128], F16,
                              kind="ExternalInput").ap()
    cos_in = nc.dram_tensor("cos_in", [RD, TOK], F16, kind="ExternalInput").ap()
    sin_in = nc.dram_tensor("sin_in", [RD, TOK], F16, kind="ExternalInput").ap()
    pt_in = nc.dram_tensor("pt_in", [RD, RD], F16, kind="ExternalInput").ap()
    par = nc.dram_tensor("par", [1, 1], I32, kind="ExternalInput").ap()
    nonce = nc.dram_tensor("nonce", [1, 1], I32, kind="ExternalInput").ap()
    xout = nc.dram_tensor("xout", [TOK, D], F32, kind="ExternalOutput").ap()
    dbgM = nc.dram_tensor("dbgM", [4, 128], F32, kind="ExternalOutput").ap()
    dbgKR = nc.dram_tensor("dbgKR", [RD, S], F16, kind="ExternalOutput").ap()
    dbgCKV = nc.dram_tensor("dbgCKV", [128, S], F16, kind="ExternalOutput").ap()
    dbgK0 = nc.dram_tensor("dbgK0", [128, S], F16, kind="ExternalOutput").ap()
    dr = {}
    for nm, shp in [("wq16", [D, D]), ("wqT", [D, D]),
                    ("wkvd16", [R_KV, D]), ("wkvdT", [D, R_KV]),
                    ("wkup16", [H * ND, R_KV]), ("wkupT", [R_KV, H * ND]),
                    ("wkr16", [RD, D]), ("wkrT", [D, RD]),
                    ("wvup16", [D, R_KV]), ("wvupT", [R_KV, D]),
                    ("wo16", [D, D]), ("woT", [D, D]),
                    ("qspill", [H * 128, TOK]), ("kspill", [H * 128, S]),
                    ("attnspill", [D, TOK])]:
        dr[nm] = nc.dram_tensor(nm, shp, F16).ap()
    T = {}
    T["shflag"] = nc.dram_tensor("shflag", [8, 16], I32, addr_space="Shared")
    ex = {
        "ckv": nc.dram_tensor("ex_ckv", [R_KV, S], F16, addr_space="Shared"),
        "kr": nc.dram_tensor("ex_kr", [RD, S], F16, addr_space="Shared"),
        "idx": 0,
    }
    ctx = ExitStack()
    with ctx:
        alloc_common(nc, ctx, T)
        alloc_attn(nc, ctx, T)
        alloc_out_stage(nc, ctx, T)
        sems = alloc_sems(nc, ctx, SEM_NAMES)
        block = ctx.enter_context(nc.Block())
        P = Plan()
        const_ok = emit_const_setup(P, T)
        pn_ok = emit_par_nonce_load(P, T, par, nonce)
        m_v = p_dma(P, "gpsimd", "gp",
                    T["amask"][:].rearrange("p a b c -> p (a b c)"), amask_in)
        c_v = p_dma(P, "gpsimd", "gp", T["cosq"][:], cos_in)
        s_v = p_dma(P, "gpsimd", "gp", T["sinq"][:], sin_in)
        p_v = p_dma(P, "gpsimd", "gp", T["PTsb"][:], pt_in)
        st_ok = emit_stream_init(P, T, x)
        done = emit_l0attn(P, T, const_ok + st_ok + pn_ok +
                           [("gp", max(m_v, c_v, s_v, p_v))],
                           ex, wq, wkvd, wkup, wkrope, wvup, wo, dr)
        p_dma(P, "gpsimd", "gp", dbgKR, ex["kr"].ap(), waits=done)
        p_dma(P, "gpsimd", "gp", dbgCKV, ex["ckv"].ap()[0:128, :], waits=done)
        p_dma(P, "gpsimd", "gp", dbgK0,
              dr["kspill"].rearrange("(hh p) t -> hh p t", p=128)[0], waits=done)
        p_dma(P, "gpsimd", "gp", dbgM[0:1, 0:64], T["qn_all"][:], waits=done)
        p_dma(P, "gpsimd", "gp", dbgM[1:2, :], T["kn_big"][:], waits=done)
        p_dma(P, "gpsimd", "gp", dbgM[2:3, 0:8], T["sc1"][0:1, :], waits=done)
        d4 = p_dma(P, "gpsimd", "gp", dbgM[3:4, 0:1], T["negMb"][0:1, :],
                   waits=done)
        emit_stream_out(P, T, xout, done + [("gp", d4)])
        P.emit(block, sems)
    return nc

def build_l0ffn_launch():
    nc = mk_nc("l0ffn")
    x = nc.dram_tensor("x", [TOK, D], F32, kind="ExternalInput").ap()
    w = nc.dram_tensor("w", [D, D], F32, kind="ExternalInput").ap()
    bias = nc.dram_tensor("bias", [128, NCH], F32, kind="ExternalInput").ap()
    xout = nc.dram_tensor("xout", [TOK, D], F32, kind="ExternalOutput").ap()
    qf16 = nc.dram_tensor("qf16", [D, D], F16)
    qT = nc.dram_tensor("qT", [D, D], F16)
    ctx = ExitStack()
    with ctx:
        T = {}
        alloc_common(nc, ctx, T)
        alloc_l0ffn(nc, ctx, T)
        alloc_out_stage(nc, ctx, T)
        sems = alloc_sems(nc, ctx, SEM_NAMES)
        block = ctx.enter_context(nc.Block())
        P = Plan()
        const_ok = emit_const_setup(P, T)
        st_ok = emit_stream_init(P, T, x)
        done = emit_l0ffn(P, T, const_ok + st_ok, w, bias, qf16.ap(), qT.ap())
        emit_stream_out(P, T, xout, done)
        P.emit(block, sems)
    return nc


def host_bias_layout(bias_1d):
    return np.ascontiguousarray(bias_1d.reshape(NCH, 128).T.astype(np.float32))


def host_amask_layout(parity):
    m = host_masks(parity)
    return np.ascontiguousarray(m.transpose(2, 0, 1, 3).reshape(128, 4 * 8 * 128))


def mk_exchange(nc, tag=""):
    ex = {}
    ex["k"] = nc.dram_tensor("ex_k" + tag, [H_KV * 128, S], F16, addr_space="Shared")
    ex["v"] = nc.dram_tensor("ex_v" + tag, [H_KV * S, 128], F16, addr_space="Shared")
    ex["kn"] = nc.dram_tensor("ex_kn" + tag, [2, 64], F32, addr_space="Shared")
    return ex


def build_l1attn_launch():
    nc = mk_nc("l1attn")
    x = nc.dram_tensor("x", [TOK, D], F32, kind="ExternalInput").ap()
    wq = nc.dram_tensor("wq", [D, D], F16, kind="ExternalInput").ap()
    wk = nc.dram_tensor("wk", [512, D], F16, kind="ExternalInput").ap()
    wv = nc.dram_tensor("wv", [512, D], F16, kind="ExternalInput").ap()
    wo = nc.dram_tensor("wo", [D, D], F16, kind="ExternalInput").ap()
    amask_in = nc.dram_tensor("amask_in", [128, 4 * 8 * 128], F16,
                              kind="ExternalInput").ap()
    par = nc.dram_tensor("par", [1, 1], I32, kind="ExternalInput").ap()
    nonce = nc.dram_tensor("nonce", [1, 1], I32, kind="ExternalInput").ap()
    xout = nc.dram_tensor("xout", [TOK, D], F32, kind="ExternalOutput").ap()
    dbgM = nc.dram_tensor("dbgM", [4, 64], F32, kind="ExternalOutput").ap()
    wqT = nc.dram_tensor("wqT", [D, D], F16)
    wkT = nc.dram_tensor("wkT", [D, 512], F16)
    wvT = nc.dram_tensor("wvT", [D, 512], F16)
    woT = nc.dram_tensor("woT", [D, D], F16)
    qspill = nc.dram_tensor("qspill", [H * 128, TOK], F16)
    T = {}
    T["shflag"] = nc.dram_tensor("shflag", [8, 16], I32, addr_space="Shared")
    ex = mk_exchange(nc)
    ex["idx"] = 0
    ctx = ExitStack()
    with ctx:
        alloc_common(nc, ctx, T)
        alloc_attn(nc, ctx, T)
        alloc_out_stage(nc, ctx, T)
        sems = alloc_sems(nc, ctx, SEM_NAMES)
        block = ctx.enter_context(nc.Block())
        P = Plan()
        const_ok = emit_const_setup(P, T)
        pn_ok = emit_par_nonce_load(P, T, par, nonce)
        m_v = p_dma(P, "gpsimd", "gp",
                    T["amask"][:].rearrange("p a b c -> p (a b c)"), amask_in)
        st_ok = emit_stream_init(P, T, x)
        done = emit_l1attn(P, T, const_ok + st_ok + pn_ok + [("gp", m_v)],
                           ex, wq, wk, wv, wo,
                           wqT.ap(), wkT.ap(), wvT.ap(), woT.ap(), qspill.ap())
        dv1 = p_dma(P, "gpsimd", "gp", dbgM[0:1, :], T["qn_all"][:], waits=done)
        p_dma(P, "gpsimd", "gp", dbgM[1:2, :], T["kn_all_b"][:], waits=done)
        p_dma(P, "gpsimd", "gp", dbgM[2:3, 0:8], T["sc1"][0:1, :], waits=done)
        dv4 = p_dma(P, "gpsimd", "gp",
                    dbgM[3:4, 0:1].broadcast_to([1, 1]), T["negMb"][0:1, :],
                    waits=done)
        emit_stream_out(P, T, xout, done + [("gp", dv4)])
        P.emit(block, sems)
    return nc




def build_mega():
    nc = mk_nc("mega")
    x = nc.dram_tensor("x", [TOK, D], F32, kind="ExternalInput").ap()
    wq0 = nc.dram_tensor("wq0", [D, D], F32, kind="ExternalInput").ap()
    wkvd = nc.dram_tensor("wkvd", [R_KV, D], F32, kind="ExternalInput").ap()
    wkup = nc.dram_tensor("wkup", [H * ND, R_KV], F32, kind="ExternalInput").ap()
    wkrope = nc.dram_tensor("wkrope", [RD, D], F32, kind="ExternalInput").ap()
    wvup = nc.dram_tensor("wvup", [D, R_KV], F32, kind="ExternalInput").ap()
    wo0 = nc.dram_tensor("wo0", [D, D], F32, kind="ExternalInput").ap()
    wf = nc.dram_tensor("wf", [D, D], F32, kind="ExternalInput").ap()
    bias = nc.dram_tensor("bias", [128, NCH], F32, kind="ExternalInput").ap()
    wq1 = nc.dram_tensor("wq1", [D, D], F16, kind="ExternalInput").ap()
    wk1 = nc.dram_tensor("wk1", [512, D], F16, kind="ExternalInput").ap()
    wv1 = nc.dram_tensor("wv1", [512, D], F16, kind="ExternalInput").ap()
    wo1 = nc.dram_tensor("wo1", [D, D], F16, kind="ExternalInput").ap()
    gw = nc.dram_tensor("gw", [NE, D], F32, kind="ExternalInput").ap()
    gu = nc.dram_tensor("gu", [G * 2 * I_FF, D], F32, kind="ExternalInput").ap()
    dn = nc.dram_tensor("dn", [G * D, I_FF], F32, kind="ExternalInput").ap()
    mgT = nc.dram_tensor("mgT_in", [32, 1], F32, kind="ExternalInput").ap()
    amask_in = nc.dram_tensor("amask_in", [128, 4 * 8 * 128], F16,
                              kind="ExternalInput").ap()
    cos_in = nc.dram_tensor("cos_in", [RD, TOK], F16, kind="ExternalInput").ap()
    sin_in = nc.dram_tensor("sin_in", [RD, TOK], F16, kind="ExternalInput").ap()
    pt_in = nc.dram_tensor("pt_in", [RD, RD], F16, kind="ExternalInput").ap()
    par = nc.dram_tensor("par", [1, 1], I32, kind="ExternalInput").ap()
    nonce = nc.dram_tensor("nonce", [1, 1], I32, kind="ExternalInput").ap()
    xout = nc.dram_tensor("xout", [TOK, D], F32, kind="ExternalOutput").ap()
    dr = {}
    for nm, shp in [("wq16", [D, D]), ("wqT", [D, D]),
                    ("wkvd16", [R_KV, D]), ("wkvdT", [D, R_KV]),
                    ("wkup16", [H * ND, R_KV]), ("wkupT", [R_KV, H * ND]),
                    ("wkr16", [RD, D]), ("wkrT", [D, RD]),
                    ("wvup16", [D, R_KV]), ("wvupT", [R_KV, D]),
                    ("wo16", [D, D]), ("woT", [D, D]),
                    ("qspill", [H * 128, TOK]), ("kspill", [H * 128, S]),
                    ("attnspill", [D, TOK]),
                    ("qf16", [D, D]), ("qT", [D, D]),
                    ("wqT1", [D, D]), ("wkT1", [D, 512]), ("wvT1", [D, 512]),
                    ("woT1", [D, D]),
                    ("gw16", [NE, D]), ("gwT", [D, NE]),
                    ("gu16", [2 * I_FF, D]), ("guT", [D, 2 * I_FF]),
                    ("dn16", [D, I_FF]), ("dnT", [I_FF, D])]:
        dr[nm] = nc.dram_tensor(nm, shp, F16).ap()
    dr["y"] = nc.dram_tensor("y_acc", [D, TOK], F32).ap()
    T = {}
    T["shflag"] = nc.dram_tensor("shflag", [8, 16], I32, addr_space="Shared")
    ex0 = {
        "ckv": nc.dram_tensor("ex_ckv", [R_KV, S], F16, addr_space="Shared"),
        "kr": nc.dram_tensor("ex_kr", [RD, S], F16, addr_space="Shared"),
        "idx": 0,
    }
    ex1 = mk_exchange(nc, tag="1")
    ex1["idx"] = 1
    ctx = ExitStack()
    with ctx:
        alloc_common(nc, ctx, T)
        alloc_attn(nc, ctx, T)
        alloc_l0ffn(nc, ctx, T)
        alloc_moe(nc, ctx, T)
        alloc_out_stage(nc, ctx, T)
        sems = alloc_sems(nc, ctx, SEM_NAMES)
        block = ctx.enter_context(nc.Block())
        P = Plan()
        const_ok = emit_const_setup(P, T)
        pn_ok = emit_par_nonce_load(P, T, par, nonce)
        m_v = p_dma(P, "gpsimd", "gp",
                    T["amask"][:].rearrange("p a b c -> p (a b c)"), amask_in)
        c_v = p_dma(P, "gpsimd", "gp", T["cosq"][:], cos_in)
        s_v = p_dma(P, "gpsimd", "gp", T["sinq"][:], sin_in)
        p_v = p_dma(P, "gpsimd", "gp", T["PTsb"][:], pt_in)
        st_ok = emit_stream_init(P, T, x)
        base_ok = const_ok + st_ok + pn_ok + [("gp", max(m_v, c_v, s_v, p_v))]
        d1 = emit_l0attn(P, T, base_ok, ex0, wq0, wkvd, wkup, wkrope, wvup,
                         wo0, dr)
        d2 = emit_l0ffn(P, T, base_ok + d1, wf, bias, dr["qf16"], dr["qT"])
        d3 = emit_l1attn(P, T, base_ok + d2, ex1, wq1, wk1, wv1, wo1,
                         dr["wqT1"], dr["wkT1"], dr["wvT1"], dr["woT1"],
                         dr["qspill"])
        emit_l1ffn(P, T, base_ok + d3, gw, gu, dn, mgT, dr, xout)
        for s_name, cntv in P.cnt.items():
            assert cntv < 60000, (s_name, cntv)
        P.emit(block, sems)
    return nc


FUSED = True


def kernel(**inputs):
    if FUSED:
        return kernel_fused(**inputs)
    return kernel_multi(**inputs)


def kernel_fused(**inputs):
    x0 = np.ascontiguousarray(inputs["L0_attn_x"], dtype=np.float32)
    shards = shard_rows(x0)
    nv = np.array([[int(np.random.randint(1, 2 ** 30))]], dtype=np.int32)
    de = np.asarray(inputs["L1_ffn_dominant_experts"]).astype(np.int64)
    gu_sel = np.ascontiguousarray(
        np.asarray(inputs["L1_ffn_gate_up_proj"], np.float32)[de]
        .reshape(G * 2 * I_FF, D))
    dn_sel = np.ascontiguousarray(
        np.asarray(inputs["L1_ffn_down_proj"], np.float32)[de]
        .reshape(G * D, I_FF))
    mgT = np.asarray(inputs["L1_ffn_merge_groups"]).astype(np.float32).reshape(32, 1)
    bias_l = host_bias_layout(np.asarray(inputs["L0_ffn_bias"], np.float32))
    nc = build_mega()
    in_maps = []
    for c in range(NCORES):
        cos, sin = host_cos_sin(c % 2)
        in_maps.append({
            "x": shards[c],
            "wq0": np.ascontiguousarray(inputs["L0_attn_wq"], np.float32),
            "wkvd": np.ascontiguousarray(inputs["L0_attn_w_kv_down"], np.float32),
            "wkup": np.ascontiguousarray(inputs["L0_attn_w_k_up"], np.float32),
            "wkrope": np.ascontiguousarray(inputs["L0_attn_w_k_rope"], np.float32),
            "wvup": np.ascontiguousarray(inputs["L0_attn_w_v_up"], np.float32),
            "wo0": np.ascontiguousarray(inputs["L0_attn_wo"], np.float32),
            "wf": np.ascontiguousarray(inputs["L0_ffn_weight"], np.float32),
            "bias": bias_l,
            "wq1": np.ascontiguousarray(inputs["L1_attn_Wq"], np.float16),
            "wk1": np.ascontiguousarray(inputs["L1_attn_Wk"], np.float16),
            "wv1": np.ascontiguousarray(inputs["L1_attn_Wv"], np.float16),
            "wo1": np.ascontiguousarray(inputs["L1_attn_Wo"], np.float16),
            "gw": np.ascontiguousarray(inputs["L1_ffn_gate_weight"], np.float32),
            "gu": gu_sel, "dn": dn_sel, "mgT_in": mgT,
            "amask_in": host_amask_layout(c % 2),
            "cos_in": cos, "sin_in": sin, "pt_in": host_perm_T(),
            "par": np.array([[c % 2]], dtype=np.int32),
            "nonce": nv,
        })
    res = run_bass_kernel_spmd(nc, in_maps, core_ids=list(range(NCORES)))
    return unshard_rows([res.results[c]["xout"] for c in range(NCORES)])


def kernel_multi(**inputs):
    x0 = np.ascontiguousarray(inputs["L0_attn_x"], dtype=np.float32)
    nonce = lambda: np.array([[int(np.random.randint(1, 2 ** 30))]], dtype=np.int32)

    host_consts = []
    for c in range(NCORES):
        cos, sin = host_cos_sin(c % 2)
        host_consts.append({
            "amask_in": host_amask_layout(c % 2),
            "cos_in": cos, "sin_in": sin, "pt_in": host_perm_T(),
            "par": np.array([[c % 2]], dtype=np.int32),
        })

    shards = shard_rows(x0)
    ncA = build_l0attn_launch()
    nvA = nonce()
    in_maps = []
    for c in range(NCORES):
        in_maps.append({
            "x": shards[c],
            "wq": np.ascontiguousarray(inputs["L0_attn_wq"], np.float32),
            "wkvd": np.ascontiguousarray(inputs["L0_attn_w_kv_down"], np.float32),
            "wkup": np.ascontiguousarray(inputs["L0_attn_w_k_up"], np.float32),
            "wkrope": np.ascontiguousarray(inputs["L0_attn_w_k_rope"], np.float32),
            "wvup": np.ascontiguousarray(inputs["L0_attn_w_v_up"], np.float32),
            "wo": np.ascontiguousarray(inputs["L0_attn_wo"], np.float32),
            "amask_in": host_consts[c]["amask_in"],
            "cos_in": host_consts[c]["cos_in"],
            "sin_in": host_consts[c]["sin_in"],
            "pt_in": host_consts[c]["pt_in"],
            "par": host_consts[c]["par"],
            "nonce": nvA,
        })
    resA = run_bass_kernel_spmd(ncA, in_maps, core_ids=list(range(NCORES)))
    x1 = [resA.results[c]["xout"] for c in range(NCORES)]

    ncB = build_l0ffn_launch()
    bias_l = host_bias_layout(np.asarray(inputs["L0_ffn_bias"], np.float32))
    in_maps = [{"x": x1[c],
                "w": np.ascontiguousarray(inputs["L0_ffn_weight"], np.float32),
                "bias": bias_l} for c in range(NCORES)]
    resB = run_bass_kernel_spmd(ncB, in_maps, core_ids=list(range(NCORES)))
    x2 = [resB.results[c]["xout"] for c in range(NCORES)]

    ncC = build_l1attn_launch()
    nvC = nonce()
    in_maps = []
    for c in range(NCORES):
        in_maps.append({
            "x": x2[c],
            "wq": np.ascontiguousarray(inputs["L1_attn_Wq"], np.float16),
            "wk": np.ascontiguousarray(inputs["L1_attn_Wk"], np.float16),
            "wv": np.ascontiguousarray(inputs["L1_attn_Wv"], np.float16),
            "wo": np.ascontiguousarray(inputs["L1_attn_Wo"], np.float16),
            "amask_in": host_consts[c]["amask_in"],
            "par": host_consts[c]["par"],
            "nonce": nvC,
        })
    resC = run_bass_kernel_spmd(ncC, in_maps, core_ids=list(range(NCORES)))
    x3 = [resC.results[c]["xout"] for c in range(NCORES)]

    ncD = build_l1ffn_launch()
    de = np.asarray(inputs["L1_ffn_dominant_experts"]).astype(np.int64)
    gu_sel = np.ascontiguousarray(
        np.asarray(inputs["L1_ffn_gate_up_proj"], np.float32)[de]
        .reshape(G * 2 * I_FF, D))
    dn_sel = np.ascontiguousarray(
        np.asarray(inputs["L1_ffn_down_proj"], np.float32)[de]
        .reshape(G * D, I_FF))
    mgT = np.asarray(inputs["L1_ffn_merge_groups"]).astype(np.float32).reshape(32, 1)
    in_maps = [{"x": x3[c],
                "gw": np.ascontiguousarray(inputs["L1_ffn_gate_weight"], np.float32),
                "gu": gu_sel, "dn": dn_sel, "mgT_in": mgT}
               for c in range(NCORES)]
    resD = run_bass_kernel_spmd(ncD, in_maps, core_ids=list(range(NCORES)))
    out_sh = [resD.results[c]["xout"] for c in range(NCORES)]
    return unshard_rows(out_sh)
